# version 25
# baseline (speedup 1.0000x reference)
import os
import numpy as np
import concourse.bass as bass
import concourse.mybir as mybir
from concourse.bass_utils import run_bass_kernel_spmd

F32 = mybir.dt.float32
BF16 = mybir.dt.bfloat16
AF = mybir.ActivationFunctionType
ALU = mybir.AluOpType
AX = mybir.AxisListType

D = 1024
T = 8192
CH = 1024
QT = 512
KC = 128
SUP = 8
NHP = 8
EPS = 1e-5
SCALE = 1.0 / 32.0


class Sem:
    def __init__(self, ctx, nc, name):
        self.h = ctx.enter_context(nc.semaphore(name))
        self.n = 0

    def inc(self, instr, k=1):
        instr.then_inc(self.h, k)
        self.n += k
        return self.n


def _w3(src):
    return src.rearrange("(c p) f -> p c f", p=128)


def build():
    nc = bass.Bass("TRN2", target_bir_lowering=False)

    d_in = {}
    for name in ["wqh", "wkh", "wvh", "wrt", "wkt", "wvt", "wo"]:
        d_in[name] = nc.dram_tensor(name, [D, D], F32, kind="ExternalInput").ap()
    d_in["xb"] = nc.dram_tensor("xb", [T, D], F32, kind="ExternalInput").ap()
    d_in["xbound"] = nc.dram_tensor("xbound", [2, D], F32, kind="ExternalInput").ap()
    d_in["mu"] = nc.dram_tensor("mu", [2, D], F32, kind="ExternalInput").ap()
    d_in["out"] = nc.dram_tensor("out", [2 * CH, D], F32, kind="ExternalOutput").ap()
    d_in["kvk"] = nc.dram_tensor("kvk", [D, T], BF16, kind="Internal").ap()
    d_in["kvv"] = nc.dram_tensor("kvv", [T, D], BF16, kind="Internal").ap()
    if os.environ.get("MKDBG"):
        d_in["dbg_xn"] = nc.dram_tensor("dbg_xn", [128, 2 * 8 * 1026], BF16, kind="ExternalOutput").ap()
        d_in["dbg_muT"] = nc.dram_tensor("dbg_muT", [128, 32], F32, kind="ExternalOutput").ap()
        d_in["dbg_q"] = nc.dram_tensor("dbg_q", [128, 8 * 512], BF16, kind="ExternalOutput").ap()
        d_in["dbg_kvk"] = nc.dram_tensor("dbg_kvk", [D, 1024], BF16, kind="ExternalOutput").ap()
        d_in["dbg_kvv"] = nc.dram_tensor("dbg_kvv", [1024, D], BF16, kind="ExternalOutput").ap()
        d_in["dbg_att"] = nc.dram_tensor("dbg_att", [128, 8 * 512], BF16, kind="ExternalOutput").ap()
        d_in["dbg_ffn"] = nc.dram_tensor("dbg_ffn", [128, 8 * 512], BF16, kind="ExternalOutput").ap()
        d_in["dbg_e"] = nc.dram_tensor("dbg_e", [128, 2 * 512], BF16, kind="ExternalOutput").ap()
        d_in["dbg_oa"] = nc.dram_tensor("dbg_oa", [128, 512], F32, kind="ExternalOutput").ap()
        d_in["dbg_ob"] = nc.dram_tensor("dbg_ob", [128, 512], F32, kind="ExternalOutput").ap()
        d_in["dbg_atm"] = nc.dram_tensor("dbg_atm", [128, 512], BF16, kind="ExternalOutput").ap()

    from contextlib import ExitStack
    with ExitStack() as ctx:
        E_ = ctx.enter_context
        t = dict(d_in)

        t["xn_own"] = E_(nc.sbuf_tensor("xn_own", [128, 2, 8, 1026], BF16))
        t["wq"] = E_(nc.sbuf_tensor("wq", [128, 8, D], BF16))
        t["wslot"] = E_(nc.sbuf_tensor("wslot", [128, 2, 8, D], BF16))
        t["qcol"] = E_(nc.sbuf_tensor("qcol", [128, 8, QT], BF16))
        t["xtm"] = E_(nc.sbuf_tensor("xtm", [128, 4, D], F32))
        t["ksup"] = E_(nc.sbuf_tensor("ksup", [128, 2, SUP * KC], BF16))
        t["vsup"] = E_(nc.sbuf_tensor("vsup", [128, 2, SUP, 130], BF16))
        t["ebufA"] = E_(nc.sbuf_tensor("ebufA", [128, 4, QT], BF16))
        t["ebufB"] = E_(nc.sbuf_tensor("ebufB", [128, 4, QT], BF16))
        t["oa"] = E_(nc.sbuf_tensor("oa", [128, QT], F32))
        t["atm"] = E_(nc.sbuf_tensor("atm", [128, 512], BF16))
        t["onesn"] = E_(nc.sbuf_tensor("onesn", [128, 64], F32))
        t["outst"] = E_(nc.sbuf_tensor("outst", [128, 4, D], F32))
        t["msk"] = E_(nc.sbuf_tensor("msk", [128, 896], BF16))
        t["idf"] = E_(nc.sbuf_tensor("idf", [128, 128], F32))
        t["idb"] = E_(nc.sbuf_tensor("idb", [128, 128], BF16))
        t["muT"] = E_(nc.sbuf_tensor("muT", [128, 4, 8], F32))
        t["st6"] = E_(nc.sbuf_tensor("st6", [128, 12], F32))
        t["stt"] = E_(nc.sbuf_tensor("stt", [128, 24], F32))
        t["mv2"] = E_(nc.sbuf_tensor("mv2", [128, 2], F32))
        t["rstd"] = E_(nc.sbuf_tensor("rstd", [128, 1], F32))
        t["rcp"] = E_(nc.sbuf_tensor("rcp", [128, 1], F32))
        t["onesf"] = E_(nc.sbuf_tensor("onesf", [1, 2], F32))
        t["onesb"] = E_(nc.sbuf_tensor("onesb", [1, 2], BF16))
        ar1 = E_(nc.sbuf_tensor("ar1", [128, 8192], BF16))
        ar2 = E_(nc.sbuf_tensor("ar2", [128, 16384], BF16))
        ar3 = E_(nc.sbuf_tensor("ar3", [128, 8192], BF16))

        t["xnring"] = ar1[:, :].rearrange("p (s c t) -> p s c t", s=2, c=8, t=512)
        t["xr_t"] = ar1[:, 0:4096].rearrange("p (c t) -> p c t", c=8, t=512)
        t["xk_t"] = ar1[:, 4096:8192].rearrange("p (c t) -> p c t", c=8, t=512)
        t["wk_t"] = ar2[:, 0:8192].rearrange("p (c f) -> p c f", c=8, f=1024)
        t["wv_t"] = ar2[:, 8192:16384].rearrange("p (c f) -> p c f", c=8, f=1024)
        t["r_t"] = ar2[:, 0:4096].rearrange("p (c t) -> p c t", c=8, t=512)
        t["kact_t"] = ar2[:, 4096:8192].rearrange("p (c t) -> p c t", c=8, t=512)
        t["kst"] = ar3[:, 0:4096].rearrange("p (c t) -> p c t", c=8, t=512)
        t["vst"] = ar3[:, 4096:8192].rearrange("p (c t) -> p c t", c=8, t=512)
        t["attncol"] = ar3[:, 0:4096].rearrange("p (c t) -> p c t", c=8, t=512)
        t["ffncol"] = ar3[:, 4096:8192].rearrange("p (c t) -> p c t", c=8, t=512)
        t["k1"] = t["oa"][:, :]
        t["rb"] = t["xtm"][:, 0, :]
        t["aev"] = t["outst"][:, 0, 0:QT]
        t["xnb0"] = ar3[0:1, 0:D]
        t["xnb1"] = ar3[0:1, D:2 * D]
        t["xbf"] = None

        t["SA"] = E_(nc.psum_tensor("SA", [128, 2 * QT], F32))
        t["SB"] = E_(nc.psum_tensor("SB", [128, 2 * QT], F32))
        t["AVa"] = E_(nc.psum_tensor("AVa", [128, QT], F32))
        t["AVb"] = E_(nc.psum_tensor("AVb", [128, QT], F32))
        t["G1"] = E_(nc.psum_tensor("G1", [128, QT], F32))
        t["G2"] = E_(nc.psum_tensor("G2", [128, 1024], BF16))

        sems = {k: Sem(ctx, nc, k) for k in
                ["s_x", "s_w", "s_kw", "s_kr", "s_mm", "s_ve", "s_ac", "s_out"]}

        engines = [nc.sync, nc.tensor, nc.vector, nc.scalar, nc.gpsimd]
        pid = nc.partition_id([e.engine for e in engines])
        idx = [pid % 4 for _ in engines]
        for p in nc.Switch(engines=engines, index=idx, n=4):
            emit_body(nc, p, sems, t)
    return nc


def emit_body(nc, p, sems, t):
    for s in sems.values():
        s.n = 0
    s_x, s_w, s_kw, s_kr = sems["s_x"], sems["s_w"], sems["s_kw"], sems["s_kr"]
    s_mm, s_ve, s_ac, s_out = sems["s_mm"], sems["s_ve"], sems["s_ac"], sems["s_out"]
    sync, tensor, vector, scalar, gpsimd = nc.sync, nc.tensor, nc.vector, nc.scalar, nc.gpsimd

    chunkA, chunkB = p, 7 - p
    PC = 8 - p
    NSUB = 2 * PC
    own_subs = {2 * chunkA: (0, 0), 2 * chunkA + 1: (0, 512),
                2 * chunkB: (1, 0), 2 * chunkB + 1: (1, 512)}

    GP = [t["G1"][:, :], t["AVa"][:, :], t["AVb"][:, :],
          t["SA"][:, 0:QT], t["SA"][:, QT:2 * QT],
          t["SB"][:, 0:QT], t["SB"][:, QT:2 * QT]]
    gp_state = {"i": 0, "free": [0] * len(GP)}
    wslot_cons = [0, 0]

    def gp_next():
        g = gp_state["i"] % len(GP)
        gp_state["i"] += 1
        return g

    def gp_mm(g, lhsT_rhs_list, out=None):
        tensor.wait_ge(s_ve.h, gp_state["free"][g])
        dst = GP[g] if out is None else out
        n = len(lhsT_rhs_list)
        for j, (lh, rh) in enumerate(lhsT_rhs_list):
            i = tensor.matmul(dst, lh, rh, start=(j == 0), stop=(j == n - 1))
        return s_mm.inc(i)

    def gp_evac(g, cp, dest, src=None):
        vector.wait_ge(s_mm.h, cp)
        i = vector.tensor_copy(dest, GP[g] if src is None else src)
        gp_state["free"][g] = s_ve.inc(i)
        return gp_state["free"][g]

    iosc = t["outst"][:, 0, 0:896]
    murow = t["outst"][0:1, 1:3, :].rearrange("p a b -> p (a b)")

    i = gpsimd.iota(iosc, [[1, 896]], channel_multiplier=-1,
                    allow_small_or_imprecise_dtypes=True)
    gpsimd.memset(t["onesf"][:, :], 1.0)
    gpsimd.memset(t["onesn"][:, :], 1.0)
    i2 = gpsimd.memset(t["onesb"][:, :], 1.0)
    s_w.inc(i2)
    cp_iota = s_w.n

    def wload(dst_ap, src_ap):
        gpsimd.dma_start(out=dst_ap, in_=src_ap).then_inc(s_w.h, 16)
        s_w.n += 16
        return s_w.n

    wload(t["wq"][:, :, :], _w3(t["wqh"]))
    wload(t["wk_t"], _w3(t["wkh"]))
    cp_w_kv = wload(t["wv_t"], _w3(t["wvh"]))
    gpsimd.dma_start(out=murow[0:1, 0:D], in_=t["mu"][0:1, :]).then_inc(s_w.h, 16)
    s_w.n += 16
    gpsimd.dma_start(out=murow[0:1, D:2 * D], in_=t["mu"][1:2, :]).then_inc(s_w.h, 16)
    s_w.n += 16
    cp_mu_dma = s_w.n

    vector.wait_ge(s_w.h, cp_iota)
    vector.tensor_scalar(t["msk"][:, :], iosc, 384.0, None, ALU.is_ge)
    vector.tensor_scalar(t["idf"][:, :], iosc[:, 0:128], 0.0, None, ALU.is_equal)
    vector.tensor_scalar(t["idb"][:, :], iosc[:, 0:128], 0.0, None, ALU.is_equal)
    for slot in range(2):
        vector.memset(t["vsup"][:, slot, :, 64:65], 1.0)
        vector.memset(t["vsup"][:, slot, :, 129:130], 1.0)
    tensor.wait_ge(s_w.h, cp_mu_dma)
    for v in range(2):
        off = v * D
        for cc in range(8):
            g = gp_next()
            cp = gp_mm(g, [(murow[0:1, off + cc * 128: off + (cc + 1) * 128],
                            t["onesf"][0:1, 0:1])], out=GP[g][:, 0:1])
            gp_evac(g, cp, t["muT"][:, v, cc:cc + 1], src=GP[g][:, 0:1])
    vector.tensor_scalar(t["muT"][:, 2, :], t["muT"][:, 0, :], -1.0, 1.0, ALU.mult, ALU.add)
    i = vector.tensor_scalar(t["muT"][:, 3, :], t["muT"][:, 1, :], -1.0, 1.0, ALU.mult, ALU.add)
    cp_mu_ve = s_ve.inc(i)

    sqjunk = t["outst"][:, 0, :]

    def ln_row(src1d, dest_bf):
        i = vector.reduce_sum(t["st6"][0:1, 0:1], src1d[0:1, :], axis=AX.X)
        s_ve.inc(i)
        scalar.wait_ge(s_ve.h, s_ve.n)
        i = scalar.activation(sqjunk[0:1, :], src1d[0:1, :], AF.Square,
                              accum_out=t["st6"][0:1, 1:2])
        s_ac.inc(i)
        vector.wait_ge(s_ac.h, s_ac.n)
        vector.tensor_scalar(t["mv2"][0:1, 0:1], t["st6"][0:1, 0:1], 1.0 / 1024.0, None, ALU.mult)
        vector.tensor_mul(t["st6"][0:1, 2:3], t["mv2"][0:1, 0:1], t["mv2"][0:1, 0:1])
        vector.tensor_scalar(t["mv2"][0:1, 1:2], t["st6"][0:1, 1:2], 1.0 / 1024.0, None, ALU.mult)
        vector.tensor_tensor(t["mv2"][0:1, 1:2], t["mv2"][0:1, 1:2], t["st6"][0:1, 2:3], ALU.subtract)
        i = vector.tensor_scalar_add(t["mv2"][0:1, 1:2], t["mv2"][0:1, 1:2], EPS)
        s_ve.inc(i)
        scalar.wait_ge(s_ve.h, s_ve.n)
        scalar.activation(t["rstd"][0:1, :], t["mv2"][0:1, 1:2], AF.Ln)
        i = scalar.activation(t["rstd"][0:1, :], t["rstd"][0:1, :], AF.Exp, scale=-0.5)
        s_ac.inc(i)
        vector.wait_ge(s_ac.h, s_ac.n)
        i = vector.tensor_scalar(dest_bf[0:1, :], src1d[0:1, :], t["mv2"][0:1, 0:1],
                                 t["rstd"][0:1, 0:1], ALU.subtract, ALU.mult)
        return s_ve.inc(i)

    sync.wait_ge(s_ve.h, cp_mu_ve)
    sync.dma_start(out=murow[0:1, 0:D], in_=t["xbound"][0:1, :]).then_inc(s_x.h, 16)
    s_x.n += 16
    sync.dma_start(out=murow[0:1, D:2 * D], in_=t["xbound"][1:2, :]).then_inc(s_x.h, 16)
    s_x.n += 16
    vector.wait_ge(s_x.h, s_x.n)
    ln_row(murow[0:1, 0:D], t["xnb0"])
    cp_b = ln_row(murow[0:1, D:2 * D], t["xnb1"])
    tensor.wait_ge(s_ve.h, cp_b)
    for ci, xnb in ((0, t["xnb0"]), (1, t["xnb1"])):
        for cc in range(8):
            g = gp_next()
            cp = gp_mm(g, [(xnb[0:1, cc * 128:(cc + 1) * 128], t["onesb"][0:1, 0:1])],
                       out=GP[g][:, 0:1])
            gp_evac(g, cp, t["xn_own"][:, ci, cc, 0:1], src=GP[g][:, 0:1])

    tensor.wait_ge(s_w.h, cp_w_kv)
    stt = t["stt"]
    g2free = [0, 0]
    stage_free = {0: 0, 1: 0}
    store_gate = {"k": 0, "v": 0}
    slotB = {}
    apply_cp = {}
    stats_cp = {}
    xn_done = {}
    dma_cp = {}

    def xn_view(s):
        if s in own_subs:
            ci, toff = own_subs[s]
            return t["xn_own"][:, ci, :, 1 + toff: 1 + toff + 512]
        return t["xnring"][:, s % 2, :, 0:512]

    def emit_dma(s):
        for j in range(4):
            if s >= 1:
                sync.wait_ge(s_mm.h, slotB[(s - 1, j)])
            sync.dma_start(out=t["xtm"][:, j, :],
                           in_=t["xb"][s * 512 + j * 128: s * 512 + (j + 1) * 128, :]
                           ).then_inc(s_x.h, 16)
            s_x.n += 16
        dma_cp[s] = s_x.n

    def emit_stats(s):
        vector.wait_ge(s_x.h, dma_cp[s])
        for j in range(4):
            i = vector.reduce_sum(stt[:, j:j + 1], t["xtm"][:, j, :], axis=AX.X)
        cp_rd = s_ve.inc(i)
        scalar.wait_ge(s_x.h, dma_cp[s])
        for j in range(4):
            i = scalar.activation(sqjunk, t["xtm"][:, j, :], AF.Square,
                                  accum_out=stt[:, 4 + j:5 + j])
        cp_sq = s_ac.inc(i)
        vector.wait_ge(s_ac.h, cp_sq)
        vector.wait_ge(s_ve.h, cp_rd)
        i = vector.tensor_scalar(stt[:, 8:12], stt[:, 0:4], 1.0 / 1024.0, None, ALU.mult)
        vector.wait_ge(s_ve.h, s_ve.inc(i))
        i = vector.tensor_mul(stt[:, 12:16], stt[:, 8:12], stt[:, 8:12])
        s_ve.inc(i)
        i = vector.tensor_scalar(stt[:, 20:24], stt[:, 4:8], 1.0 / 1024.0, None, ALU.mult)
        vector.wait_ge(s_ve.h, s_ve.inc(i))
        i = vector.tensor_tensor(stt[:, 12:16], stt[:, 20:24], stt[:, 12:16], ALU.subtract)
        vector.wait_ge(s_ve.h, s_ve.inc(i))
        i = vector.tensor_scalar_add(stt[:, 12:16], stt[:, 12:16], EPS)
        cp_v = s_ve.inc(i)
        scalar.wait_ge(s_ve.h, cp_v)
        scalar.activation(stt[:, 16:20], stt[:, 12:16], AF.Ln)
        i = scalar.activation(stt[:, 16:20], stt[:, 16:20], AF.Exp, scale=-0.5)
        stats_cp[s] = s_ac.inc(i)

    def emit_apply(s, tt):
        slot = tt % 2
        stage = (t["ebufA"] if slot == 0 else t["ebufB"])[:, 0:2, :].rearrange("p a b -> p (a b)")
        vector.wait_ge(s_ac.h, stats_cp[s])
        vector.wait_ge(s_mm.h, stage_free[slot])
        i = vector.tensor_scalar(stage, t["xtm"][:, tt, :], stt[:, 8 + tt:9 + tt],
                                 stt[:, 16 + tt:17 + tt], ALU.subtract, ALU.mult)
        apply_cp[(s, tt)] = s_ve.inc(i)

    def emit_T(s, tt):
        slot = tt % 2
        stage = (t["ebufA"] if slot == 0 else t["ebufB"])[:, 0:2, :].rearrange("p a b -> p (a b)")
        dv = xn_view(s)[:, :, tt * 128:(tt + 1) * 128]
        tensor.wait_ge(s_ve.h, max(apply_cp[(s, tt)], g2free[0]))
        for j in range(8):
            i = tensor.transpose(t["G2"][:, j * 128:(j + 1) * 128],
                                 stage[:, j * 128:(j + 1) * 128], t["idb"][:, :])
        cpB = s_mm.inc(i)
        slotB[(s, tt)] = cpB
        stage_free[slot] = cpB
        vector.wait_ge(s_mm.h, cpB)
        i = vector.tensor_copy(dv[:, :, :],
                               t["G2"][:, :].rearrange("p (c f) -> p c f", c=8, f=128))
        g2free[0] = s_ve.inc(i)
        if tt == 3:
            xn_done[s] = g2free[0]
        if tt == 0:
            emit_apply(s, 2)
        elif tt == 1:
            emit_apply(s, 3)

    def emit_kv_group(sI, g):
        xnsrc = xn_view(sI)
        if g == 0:
            tensor.wait_ge(s_ve.h, xn_done[sI])
        if g < 8:
            fj = g
            gidx = gp_next()
            cp = gp_mm(gidx, [(t["wk_t"][:, cc, fj * 128:(fj + 1) * 128], xnsrc[:, cc, :])
                              for cc in range(8)])
            if g == 0:
                vector.wait_ge(s_kw.h, store_gate["k"])
            cpk = gp_evac(gidx, cp, t["kst"][:, fj, :])
            if g == 7:
                gpsimd.wait_ge(s_ve.h, cpk)
                gpsimd.dma_start(
                    out=t["kvk"].rearrange("(c p) t -> p c t", p=128)[:, :, sI * 512:(sI + 1) * 512],
                    in_=t["kst"][:, :, :]).then_inc(s_kw.h, 16)
                s_kw.n += 16
                store_gate["k"] = s_kw.n
        else:
            tj, dh = (g - 8) // 2, (g - 8) % 2
            gidx = gp_next()
            cp = gp_mm(gidx, [(xnsrc[:, cc, tj * 128:(tj + 1) * 128],
                               t["wv_t"][:, cc, dh * 512:(dh + 1) * 512])
                              for cc in range(8)])
            if g == 8:
                vector.wait_ge(s_kw.h, store_gate["v"])
            cpv = gp_evac(gidx, cp, t["vst"][:, tj * 2 + dh, :])
            if g == 15:
                gpsimd.wait_ge(s_ve.h, cpv)
                gpsimd.dma_start(
                    out=t["kvv"][sI * 512:(sI + 1) * 512, :].rearrange(
                        "(j p) (h f) -> p j h f", p=128, h=2),
                    in_=t["vst"][:, :, :].rearrange("p (j h) f -> p j h f", h=2)
                ).then_inc(s_kw.h, 16)
                s_kw.n += 16
                store_gate["v"] = s_kw.n

    emit_dma(0)
    emit_stats(0)
    emit_apply(0, 0)
    emit_apply(0, 1)
    for it in range(NSUB + 1):
        sT, sKV = it, it - 1
        if sKV < 0:
            for tt in range(4):
                emit_T(sT, tt)
        elif sT < NSUB:
            for g in range(16):
                emit_kv_group(sKV, g)
                if g in (3, 6, 9, 12):
                    emit_T(sT, (3, 6, 9, 12).index(g))
        else:
            for g in range(16):
                emit_kv_group(sKV, g)
        if sT + 1 <= NSUB - 1:
            emit_dma(sT + 1)
            emit_stats(sT + 1)
            emit_apply(sT + 1, 0)
            emit_apply(sT + 1, 1)

    if os.environ.get("MKDBG"):
        sync.wait_ge(s_ve.h, xn_done[NSUB - 1])
        sync.dma_start(out=t["dbg_xn"][:, :],
                       in_=t["xn_own"][:, :, :, :].rearrange("p a b c -> p (a b c)")
                       ).then_inc(s_out.h, 16)
        s_out.n += 16
        sync.dma_start(out=t["dbg_muT"][:, :],
                       in_=t["muT"][:, :, :].rearrange("p a b -> p (a b)")).then_inc(s_out.h, 16)
        s_out.n += 16

    xslot_free = [0] * 4
    cp_ph2_mm = s_mm.n
    gpsimd.wait_ge(s_mm.h, cp_ph2_mm)
    cp_wvt_res = wload(t["wv_t"], _w3(t["wvt"]))
    mix_gate = [cp_ph2_mm]
    cp_kv_done = s_kw.n
    sync.wait_ge(s_kw.h, cp_kv_done)
    if os.environ.get("MKDBG"):
        sync.dma_start(out=t["dbg_kvk"][:, :], in_=t["kvk"][:, 0:1024]).then_inc(s_out.h, 16)
        s_out.n += 16
        sync.dma_start(out=t["dbg_kvv"][:, :], in_=t["kvv"][0:1024, :]).then_inc(s_out.h, 16)
        s_out.n += 16

    norm_state = {"ve": 0, "sh": s_kw.n, "bc": 0, "pend_rc": None, "pend": None}
    sup = {"last": [0, 0], "cp": {}, "base": {}}
    _run = 0
    for q_ in range(4):
        _ch = (chunkA if q_ < 2 else chunkB)
        _ns = ((_ch * CH + (q_ % 2) * 512) // KC + 4 + SUP - 1) // SUP
        for h_ in range(NHP):
            sup["base"][(q_, h_)] = _run % 2
            _run += _ns

    def issue_sup_g(q_, h_, ks):
        slot = (sup["base"][(q_, h_)] + ks) % 2
        sync.wait_ge(s_mm.h, sup["last"][slot])
        sync.dma_start(out=t["ksup"][:, slot, :],
                       in_=t["kvk"][h_ * 128:(h_ + 1) * 128,
                                    ks * 1024:(ks + 1) * 1024]).then_inc(s_kr.h, 16)
        s_kr.n += 16
        for b_ in range(2):
            sync.dma_start(
                out=t["vsup"][:, slot, :, b_ * 65: b_ * 65 + 64],
                in_=t["kvv"][ks * 1024:(ks + 1) * 1024,
                             h_ * 128 + b_ * 64: h_ * 128 + b_ * 64 + 64]
                    .rearrange("(a p) f -> p a f", p=128)).then_inc(s_kr.h, 16)
            s_kr.n += 16
        sup["cp"][(q_, h_, ks)] = s_kr.n

    for qt in range(4):
        qt_out_gate = s_out.n
        ci = 0 if qt < 2 else 1
        chunk = chunkA if qt < 2 else chunkB
        toff = (qt % 2) * 512
        q0 = chunk * CH + toff
        kc_end = q0 // KC + 4
        nsup = (kc_end + SUP - 1) // SUP
        xcur = t["xn_own"][:, ci, :, 1 + toff: 1 + toff + 512]
        xprev = t["xn_own"][:, ci, :, toff: toff + 512]

        for fj in range(8):
            g = gp_next()
            cp = gp_mm(g, [(t["wq"][:, cc, fj * 128:(fj + 1) * 128], xcur[:, cc, :])
                           for cc in range(8)])
            cp_q = gp_evac(g, cp, t["qcol"][:, fj, :])

        if os.environ.get("MKDBG") and qt == 0:
            sync.wait_ge(s_ve.h, cp_q)
            sync.dma_start(out=t["dbg_q"][:, :],
                           in_=t["qcol"][:, :, :].rearrange("p a b -> p (a b)")).then_inc(s_out.h, 16)
            s_out.n += 16

        vector.wait_ge(s_mm.h, mix_gate[0])
        for cc in range(8):
            vector.tensor_scalar(t["xr_t"][:, cc, :], xcur[:, cc, :],
                                 t["muT"][:, 0, cc:cc + 1], None, ALU.mult)
            vector.scalar_tensor_tensor(t["xr_t"][:, cc, :], xprev[:, cc, :],
                                        t["muT"][:, 2, cc:cc + 1], t["xr_t"][:, cc, :],
                                        ALU.mult, ALU.add)
            vector.tensor_scalar(t["xk_t"][:, cc, :], xcur[:, cc, :],
                                 t["muT"][:, 1, cc:cc + 1], None, ALU.mult)
            i = vector.scalar_tensor_tensor(t["xk_t"][:, cc, :], xprev[:, cc, :],
                                            t["muT"][:, 3, cc:cc + 1], t["xk_t"][:, cc, :],
                                            ALU.mult, ALU.add)
        cp_mix = s_ve.inc(i)
        gpsimd.wait_ge(s_mm.h, wslot_cons[0])
        cp_wrt = wload(t["wslot"][:, 0, :, :], _w3(t["wrt"]))
        gpsimd.wait_ge(s_mm.h, wslot_cons[1])
        cp_wkt = wload(t["wslot"][:, 1, :, :], _w3(t["wkt"]))

        for hp in range(NHP):
            sc_cp = {}
            av_cp = {}
            e_cp = {}
            sbase = sup["base"][(qt, hp)]

            if (qt, hp, 0) not in sup["cp"]:
                issue_sup_g(qt, hp, 0)
            if nsup > 1 and (qt, hp, 1) not in sup["cp"]:
                issue_sup_g(qt, hp, 1)
            tensor.wait_ge(s_ve.h, cp_q)

            def em_scores(kc):
                half = kc % 2
                ks = kc // SUP
                tensor.wait_ge(s_kr.h, sup["cp"][(qt, hp, ks)])
                if kc == 2 and norm_state["pend"]:
                    tensor.wait_ge(s_ve.h, norm_state["pend"][1])
                if kc >= 2:
                    tensor.wait_ge(s_ac.h, e_cp[kc - 2][2])
                slot = (sbase + ks) % 2
                tensor.matmul(t["SA"][:, half * QT:(half + 1) * QT],
                              t["ksup"][0:64, slot, (kc % SUP) * KC:(kc % SUP + 1) * KC],
                              t["qcol"][0:64, hp, :], start=True, stop=True)
                i = tensor.matmul(t["SB"][:, half * QT:(half + 1) * QT],
                                  t["ksup"][64:128, slot, (kc % SUP) * KC:(kc % SUP + 1) * KC],
                                  t["qcol"][64:128, hp, :], start=True, stop=True)
                sc_cp[kc] = s_mm.inc(i)

            def em_exp_pair(kc0):
                eh0 = kc0 % 4
                scalar.wait_ge(s_mm.h, sc_cp[kc0 + 1])
                if kc0 >= 4:
                    scalar.wait_ge(s_mm.h, av_cp[kc0 - 3])
                dstA = t["ebufA"][:, eh0:eh0 + 2, :].rearrange("p a b -> p (a b)")
                dstB = t["ebufB"][:, eh0:eh0 + 2, :].rearrange("p a b -> p (a b)")
                scalar.activation(dstA, t["SA"][:, :], AF.Exp, scale=SCALE)
                i = scalar.activation(dstB, t["SB"][:, :], AF.Exp, scale=SCALE)
                ac = s_ac.inc(i)
                for kc in (kc0, kc0 + 1):
                    eh = kc % 4
                    c = kc * KC - q0
                    if 0 <= c < 512:
                        vector.wait_ge(s_ac.h, ac)
                        m = t["msk"][:, 384 - c: 896 - c]
                        vector.tensor_tensor(t["ebufA"][:, eh, :], t["ebufA"][:, eh, :], m, ALU.mult)
                        i = vector.tensor_tensor(t["ebufB"][:, eh, :], t["ebufB"][:, eh, :], m, ALU.mult)
                        e_cp[kc] = ("ve", s_ve.inc(i), ac)
                    else:
                        e_cp[kc] = ("ac", ac, ac)

            def em_av(kc):
                eh = kc % 4
                kind, cpv, _ = e_cp[kc]
                tensor.wait_ge(s_ac.h if kind == "ac" else s_ve.h, cpv)
                ks = kc // SUP
                slot = (sbase + ks) % 2
                vk = t["vsup"][:, slot, kc % SUP, :]
                tensor.matmul(t["AVa"][0:65, :], vk[:, 0:65], t["ebufA"][:, eh, :],
                              start=(kc == 0), stop=(kc == kc_end - 1))
                i = tensor.matmul(t["AVb"][0:65, :], vk[:, 65:130], t["ebufB"][:, eh, :],
                                  start=(kc == 0), stop=(kc == kc_end - 1))
                av_cp[kc] = s_mm.inc(i)
                sup["last"][slot] = av_cp[kc]
                if (kc + 1) % SUP == 0:
                    ks_next = kc // SUP + 2
                    if ks_next < nsup:
                        issue_sup_g(qt, hp, ks_next)

            em_scores(0)
            em_scores(1)
            em_exp_pair(0)
            def emit_norm_tail():
                if norm_state["pend_rc"] is None:
                    return
                php, cp_rc = norm_state["pend_rc"]
                norm_state["pend_rc"] = None
                tensor.wait_ge(s_ve.h, cp_rc)
                tensor.matmul(t["G1"][0:64, :], t["onesn"][64:65, :],
                              t["rb"][64:65, 0:512], start=True, stop=True)
                i = tensor.matmul(t["SA"][0:64, 0:QT], t["onesn"][64:65, :],
                                  t["rb"][64:65, 512:1024], start=True, stop=True)
                cp_bc = s_mm.inc(i)
                vector.wait_ge(s_mm.h, cp_bc)
                i = vector.tensor_tensor(t["attncol"][0:64, php, :], t["oa"][0:64, :],
                                         t["G1"][0:64, :], ALU.mult)
                cp_mA = s_ve.inc(i)
                vector.wait_ge(s_kw.h, norm_state["sh"])
                i = vector.tensor_tensor(t["atm"][0:64, :], t["aev"][0:64, :],
                                         t["SA"][0:64, 0:QT], ALU.mult)
                cp_mB = s_ve.inc(i)
                gpsimd.wait_ge(s_ve.h, cp_mB)
                gpsimd.dma_start(out=t["attncol"][64:128, php, :],
                                 in_=t["atm"][0:64, :]).then_inc(s_kw.h, 16)
                s_kw.n += 16
                norm_state["sh"] = s_kw.n
                norm_state["bc"] = cp_bc
                norm_state["pend"] = (True, cp_mB)
                gp_state["free"][0] = cp_mA
                gp_state["free"][3] = cp_mB

            av_gate_done = False
            for kc in range(kc_end):
                if not av_gate_done:
                    tensor.wait_ge(s_ve.h, norm_state["ve"])
                    av_gate_done = True
                em_av(kc)
                if kc == 0:
                    emit_norm_tail()
                if kc + 2 < kc_end:
                    em_scores(kc + 2)
                    if kc % 2 == 1:
                        em_exp_pair(kc + 1)

            vector.wait_ge(s_mm.h, av_cp[kc_end - 1])
            if hp == 0:
                vector.wait_ge(s_out.h, qt_out_gate)
            vector.reciprocal(t["rb"][64:65, 0:512], t["AVa"][64:65, :])
            i = vector.reciprocal(t["rb"][64:65, 512:1024], t["AVb"][64:65, :])
            norm_state["pend_rc"] = (hp, s_ve.inc(i))
            vector.tensor_copy(t["oa"][0:65, :], t["AVa"][0:65, :])
            i = vector.tensor_copy(t["aev"][0:65, :], t["AVb"][0:65, :])
            cp_ev = s_ve.inc(i)
            norm_state["ve"] = cp_ev
            gp_state["free"][1] = cp_ev
            gp_state["free"][2] = cp_ev
            if hp + 1 < NHP:
                issue_sup_g(qt, hp + 1, 0)
            elif qt + 1 < 4:
                issue_sup_g(qt + 1, 0, 0)
        emit_norm_tail()
        cp_attn = s_ve.n
        if os.environ.get("MKDBG") and qt == 0:
            sync.wait_ge(s_ve.h, cp_attn)
            sync.dma_start(out=t["dbg_att"][:, :],
                           in_=t["attncol"][:, :, :].rearrange("p a b -> p (a b)")).then_inc(s_out.h, 16)
            s_out.n += 16
            sync.dma_start(out=t["dbg_e"][:, :],
                           in_=t["ebufA"][:, 0:2, :].rearrange("p a b -> p (a b)")).then_inc(s_out.h, 16)
            s_out.n += 16

        tensor.wait_ge(s_ve.h, cp_mix)
        tensor.wait_ge(s_w.h, cp_wrt)
        for dj in range(8):
            g = gp_next()
            cp = gp_mm(g, [(t["wslot"][:, 0, cc, dj * 128:(dj + 1) * 128], t["xr_t"][:, cc, :])
                           for cc in range(8)])
            scalar.wait_ge(s_mm.h, cp)
            i = scalar.activation(t["r_t"][:, dj, :], GP[g], AF.Sigmoid)
            ac = s_ac.inc(i)
            vector.wait_ge(s_ac.h, ac)
            i = vector.tensor_scalar(t["rcp"][:, 0:1], t["rcp"][:, 0:1], 1.0, None, ALU.mult)
            gp_state["free"][g] = s_ve.inc(i)
        cp_r_mm = s_mm.n

        gpsimd.wait_ge(s_mm.h, cp_r_mm)
        cp_wo = wload(t["wslot"][:, 0, :, :], _w3(t["wo"]))

        tensor.wait_ge(s_w.h, cp_wkt)
        for dj in range(8):
            g = gp_next()
            cp = gp_mm(g, [(t["wslot"][:, 1, cc, dj * 128:(dj + 1) * 128], t["xk_t"][:, cc, :])
                           for cc in range(8)])
            vector.wait_ge(s_mm.h, cp)
            vector.tensor_scalar_max(t["k1"][:, :], GP[g], 0.0)
            i = vector.tensor_mul(t["kact_t"][:, dj, :], t["k1"][:, :], t["k1"][:, :])
            gp_state["free"][g] = s_ve.inc(i)
        cp_k_mm = s_mm.n
        wslot_cons[1] = cp_k_mm
        mix_gate[0] = cp_k_mm

        tensor.wait_ge(s_w.h, cp_wvt_res)
        for dj in range(8):
            g = gp_next()
            cp = gp_mm(g, [(t["wv_t"][:, cc, dj * 128:(dj + 1) * 128], t["kact_t"][:, cc, :])
                           for cc in range(8)])
            vector.wait_ge(s_mm.h, cp)
            i = vector.tensor_tensor(t["ffncol"][:, dj, :], GP[g], t["r_t"][:, dj, :], ALU.mult)
            gp_state["free"][g] = s_ve.inc(i)

        if os.environ.get("MKDBG") and qt == 0:
            sync.wait_ge(s_ve.h, s_ve.n)
            sync.dma_start(out=t["dbg_ffn"][:, :],
                           in_=t["ffncol"][:, :, :].rearrange("p a b -> p (a b)")).then_inc(s_out.h, 16)
            s_out.n += 16

        sync.wait_ge(s_mm.h, norm_state["bc"])
        for j in range(4):
            sync.wait_ge(s_ve.h, xslot_free[j])
            sync.dma_start(out=t["xtm"][:, j, :],
                           in_=t["xb"][q0 + j * 128: q0 + (j + 1) * 128, :]).then_inc(s_x.h, 16)
            s_x.n += 16
        cp_xrr = s_x.n

        cp_out_prev = s_out.n
        vector.wait_ge(s_out.h, cp_out_prev)
        tensor.wait_ge(s_ve.h, cp_attn)
        tensor.wait_ge(s_kw.h, norm_state["sh"])
        tensor.wait_ge(s_w.h, cp_wo)
        for fo in range(8):
            g = gp_next()
            cp = gp_mm(g, [(t["wslot"][:, 0, cc, fo * 128:(fo + 1) * 128], t["attncol"][:, cc, :])
                           for cc in range(8)])
            vector.wait_ge(s_mm.h, cp)
            i = vector.tensor_tensor(t["k1"][:, :], GP[g], t["ffncol"][:, fo, :], ALU.add)
            gp_state["free"][g] = s_ve.inc(i)
            tensor.wait_ge(s_ve.h, s_ve.n)
            for j in range(4):
                i = tensor.transpose(t["G1"][:, j * 128:(j + 1) * 128],
                                     t["k1"][:, j * 128:(j + 1) * 128], t["idf"][:, :])
            cp2 = s_mm.inc(i)
            vector.wait_ge(s_mm.h, cp2)
            vector.wait_ge(s_x.h, cp_xrr)
            i = vector.tensor_tensor(
                t["outst"][:, :, fo * 128:(fo + 1) * 128],
                t["G1"][:, :].rearrange("p (j f) -> p j f", j=4),
                t["xtm"][:, :, fo * 128:(fo + 1) * 128], ALU.add)
            s_ve.inc(i)
            tensor.wait_ge(s_ve.h, s_ve.n)
        wslot_cons[0] = s_mm.n
        cp_comb = s_ve.n
        sync.wait_ge(s_ve.h, cp_comb)
        sync.dma_start(
            out=t["out"][qt * 512:(qt + 1) * 512, :].rearrange("(j p) f -> p j f", p=128),
            in_=t["outst"][:, :, :]).then_inc(s_out.h, 16)
        s_out.n += 16
        for j in range(4):
            xslot_free[j] = cp_comb

    sync.wait_ge(s_out.h, s_out.n)


_NC_CACHE = [None]


def kernel(L0_attn_x, L0_attn_w_qkv, L0_attn_w_o, L0_ffn_W_r, L0_ffn_W_k,
           L0_ffn_W_v, L0_ffn_mu_r, L0_ffn_mu_k):
    x = np.ascontiguousarray(np.asarray(L0_attn_x, dtype=np.float32))
    w_qkv = np.asarray(L0_attn_w_qkv, dtype=np.float32)
    NH, HD = 16, 64
    qkv3 = w_qkv.reshape(D, NH, 3 * HD)
    wqh = np.ascontiguousarray(qkv3[:, :, 0:HD].reshape(D, D))
    wkh = np.ascontiguousarray(qkv3[:, :, HD:2 * HD].reshape(D, D))
    wvh = np.ascontiguousarray(qkv3[:, :, 2 * HD:3 * HD].reshape(D, D))
    wrt = np.ascontiguousarray(np.asarray(L0_ffn_W_r, np.float32).T)
    wkt = np.ascontiguousarray(np.asarray(L0_ffn_W_k, np.float32).T)
    wvt = np.ascontiguousarray(np.asarray(L0_ffn_W_v, np.float32).T)
    wo_ = np.ascontiguousarray(np.asarray(L0_attn_w_o, np.float32))
    mu_ = np.ascontiguousarray(
        np.stack([np.asarray(L0_ffn_mu_r, np.float32), np.asarray(L0_ffn_mu_k, np.float32)]))

    if _NC_CACHE[0] is None:
        _NC_CACHE[0] = build()
    nc = _NC_CACHE[0]

    in_maps = []
    for i in range(8):
        b, p = i // 4, i % 4
        a0, b0 = p * CH, (7 - p) * CH
        xbound = np.zeros((2, D), np.float32)
        if a0 > 0:
            xbound[0] = x[b, a0 - 1]
        xbound[1] = x[b, b0 - 1]
        in_maps.append(dict(
            xb=x[b], xbound=xbound, wqh=wqh, wkh=wkh, wvh=wvh,
            wrt=wrt, wkt=wkt, wvt=wvt, wo=wo_, mu=mu_))
    res = run_bass_kernel_spmd(nc, in_maps, core_ids=list(range(8)))
    outp = np.empty((2, T, D), np.float32)
    for i in range(8):
        b, p = i // 4, i % 4
        o = res.results[i]["out"]
        outp[b, p * CH:(p + 1) * CH] = o[0:CH]
        outp[b, (7 - p) * CH:(8 - p) * CH] = o[CH:2 * CH]
    return outp


# version 26
# speedup vs baseline: 1.5397x; 1.5397x over previous
import os
import numpy as np
import concourse.bass as bass
import concourse.mybir as mybir
from concourse.bass_utils import run_bass_kernel_spmd

F32 = mybir.dt.float32
BF16 = mybir.dt.bfloat16
AF = mybir.ActivationFunctionType
ALU = mybir.AluOpType
AX = mybir.AxisListType

D = 1024
T = 8192
CH = 1024
QT = 512
KC = 128
SUP = 8
NHP = 8
EPS = 1e-5
SCALE = 1.0 / 32.0


class Sem:
    def __init__(self, ctx, nc, name):
        self.h = ctx.enter_context(nc.semaphore(name))
        self.n = 0

    def inc(self, instr, k=1):
        instr.then_inc(self.h, k)
        self.n += k
        return self.n


def _w3(src):
    return src.rearrange("(c p) f -> p c f", p=128)


def build():
    nc = bass.Bass("TRN2", target_bir_lowering=False)

    d_in = {}
    for name in ["wqh", "wkh", "wvh", "wrt", "wkt", "wvt", "wo"]:
        d_in[name] = nc.dram_tensor(name, [D, D], F32, kind="ExternalInput").ap()
    d_in["xb"] = nc.dram_tensor("xb", [T, D], F32, kind="ExternalInput").ap()
    d_in["xbound"] = nc.dram_tensor("xbound", [2, D], F32, kind="ExternalInput").ap()
    d_in["mu"] = nc.dram_tensor("mu", [2, D], F32, kind="ExternalInput").ap()
    d_in["out"] = nc.dram_tensor("out", [2 * CH, D], F32, kind="ExternalOutput").ap()
    d_in["kvk"] = nc.dram_tensor("kvk", [D, T], BF16, kind="Internal").ap()
    d_in["kvv"] = nc.dram_tensor("kvv", [T, D], BF16, kind="Internal").ap()
    if os.environ.get("MKDBG"):
        d_in["dbg_xn"] = nc.dram_tensor("dbg_xn", [128, 2 * 8 * 1026], BF16, kind="ExternalOutput").ap()
        d_in["dbg_muT"] = nc.dram_tensor("dbg_muT", [128, 32], F32, kind="ExternalOutput").ap()
        d_in["dbg_q"] = nc.dram_tensor("dbg_q", [128, 8 * 512], BF16, kind="ExternalOutput").ap()
        d_in["dbg_kvk"] = nc.dram_tensor("dbg_kvk", [D, 1024], BF16, kind="ExternalOutput").ap()
        d_in["dbg_kvv"] = nc.dram_tensor("dbg_kvv", [1024, D], BF16, kind="ExternalOutput").ap()
        d_in["dbg_att"] = nc.dram_tensor("dbg_att", [128, 8 * 512], BF16, kind="ExternalOutput").ap()
        d_in["dbg_ffn"] = nc.dram_tensor("dbg_ffn", [128, 8 * 512], BF16, kind="ExternalOutput").ap()
        d_in["dbg_e"] = nc.dram_tensor("dbg_e", [128, 2 * 512], BF16, kind="ExternalOutput").ap()
        d_in["dbg_oa"] = nc.dram_tensor("dbg_oa", [128, 512], F32, kind="ExternalOutput").ap()
        d_in["dbg_ob"] = nc.dram_tensor("dbg_ob", [128, 512], F32, kind="ExternalOutput").ap()
        d_in["dbg_atm"] = nc.dram_tensor("dbg_atm", [128, 512], BF16, kind="ExternalOutput").ap()

    from contextlib import ExitStack
    with ExitStack() as ctx:
        E_ = ctx.enter_context
        t = dict(d_in)

        t["xn_own"] = E_(nc.sbuf_tensor("xn_own", [128, 2, 8, 1026], BF16))
        t["wq"] = E_(nc.sbuf_tensor("wq", [128, 8, D], BF16))
        t["wslot"] = E_(nc.sbuf_tensor("wslot", [128, 2, 8, D], BF16))
        t["qcol"] = E_(nc.sbuf_tensor("qcol", [128, 8, QT], BF16))
        t["xtm"] = E_(nc.sbuf_tensor("xtm", [128, 4, D], F32))
        t["ksup"] = E_(nc.sbuf_tensor("ksup", [128, 2, SUP * KC], BF16))
        t["vsup"] = E_(nc.sbuf_tensor("vsup", [128, 2, SUP, 130], BF16))
        t["ebufAB"] = E_(nc.sbuf_tensor("ebufAB", [128, 4, 2 * QT], BF16))
        t["oa"] = E_(nc.sbuf_tensor("oa", [128, QT], F32))
        t["atm"] = E_(nc.sbuf_tensor("atm", [128, 512], BF16))
        t["onesn"] = E_(nc.sbuf_tensor("onesn", [128, 64], F32))
        t["outst"] = E_(nc.sbuf_tensor("outst", [128, 4, D], F32))
        t["msk"] = E_(nc.sbuf_tensor("msk", [128, 896], BF16))
        t["idf"] = E_(nc.sbuf_tensor("idf", [128, 128], F32))
        t["idb"] = E_(nc.sbuf_tensor("idb", [128, 128], BF16))
        t["muT"] = E_(nc.sbuf_tensor("muT", [128, 4, 8], F32))
        t["st6"] = E_(nc.sbuf_tensor("st6", [128, 12], F32))
        t["stt"] = E_(nc.sbuf_tensor("stt", [128, 24], F32))
        t["mv2"] = E_(nc.sbuf_tensor("mv2", [128, 2], F32))
        t["rstd"] = E_(nc.sbuf_tensor("rstd", [128, 1], F32))
        t["rcp"] = E_(nc.sbuf_tensor("rcp", [128, 1], F32))
        t["onesf"] = E_(nc.sbuf_tensor("onesf", [1, 2], F32))
        t["onesb"] = E_(nc.sbuf_tensor("onesb", [1, 2], BF16))
        ar1 = E_(nc.sbuf_tensor("ar1", [128, 8192], BF16))
        ar2 = E_(nc.sbuf_tensor("ar2", [128, 16384], BF16))
        ar3 = E_(nc.sbuf_tensor("ar3", [128, 8192], BF16))

        t["xnring"] = ar1[:, :].rearrange("p (s c t) -> p s c t", s=2, c=8, t=512)
        t["xr_t"] = ar1[:, 0:4096].rearrange("p (c t) -> p c t", c=8, t=512)
        t["xk_t"] = ar1[:, 4096:8192].rearrange("p (c t) -> p c t", c=8, t=512)
        t["wk_t"] = ar2[:, 0:8192].rearrange("p (c f) -> p c f", c=8, f=1024)
        t["wv_t"] = ar2[:, 8192:16384].rearrange("p (c f) -> p c f", c=8, f=1024)
        t["r_t"] = ar2[:, 0:4096].rearrange("p (c t) -> p c t", c=8, t=512)
        t["kact_t"] = ar2[:, 4096:8192].rearrange("p (c t) -> p c t", c=8, t=512)
        t["kst"] = ar3[:, 0:4096].rearrange("p (c t) -> p c t", c=8, t=512)
        t["vst"] = ar3[:, 4096:8192].rearrange("p (c t) -> p c t", c=8, t=512)
        t["attncol"] = ar3[:, 0:4096].rearrange("p (c t) -> p c t", c=8, t=512)
        t["ffncol"] = ar3[:, 4096:8192].rearrange("p (c t) -> p c t", c=8, t=512)
        t["k1"] = t["oa"][:, :]
        t["rb"] = t["xtm"][:, 0, :]
        t["aev"] = t["outst"][:, 0, 0:QT]
        t["xnb0"] = ar3[0:1, 0:D]
        t["xnb1"] = ar3[0:1, D:2 * D]
        t["xbf"] = None

        t["SAB"] = E_(nc.psum_tensor("SAB", [128, 4 * QT], F32))
        t["AVa"] = E_(nc.psum_tensor("AVa", [128, QT], F32))
        t["AVb"] = E_(nc.psum_tensor("AVb", [128, QT], F32))
        t["G1"] = E_(nc.psum_tensor("G1", [128, QT], F32))
        t["G2"] = E_(nc.psum_tensor("G2", [128, 1024], BF16))

        sems = {k: Sem(ctx, nc, k) for k in
                ["s_x", "s_w", "s_kw", "s_kr", "s_mm", "s_ve", "s_ac", "s_out"]}

        engines = [nc.sync, nc.tensor, nc.vector, nc.scalar, nc.gpsimd]
        pid = nc.partition_id([e.engine for e in engines])
        idx = [pid % 4 for _ in engines]
        for p in nc.Switch(engines=engines, index=idx, n=4):
            emit_body(nc, p, sems, t)
    return nc


def emit_body(nc, p, sems, t):
    for s in sems.values():
        s.n = 0
    s_x, s_w, s_kw, s_kr = sems["s_x"], sems["s_w"], sems["s_kw"], sems["s_kr"]
    s_mm, s_ve, s_ac, s_out = sems["s_mm"], sems["s_ve"], sems["s_ac"], sems["s_out"]
    sync, tensor, vector, scalar, gpsimd = nc.sync, nc.tensor, nc.vector, nc.scalar, nc.gpsimd

    chunkA, chunkB = p, 7 - p
    PC = 8 - p
    NSUB = 2 * PC
    own_subs = {2 * chunkA: (0, 0), 2 * chunkA + 1: (0, 512),
                2 * chunkB: (1, 0), 2 * chunkB + 1: (1, 512)}

    GP = [t["G1"][:, :], t["AVa"][:, :], t["AVb"][:, :],
          t["SAB"][:, 0:QT], t["SAB"][:, QT:2 * QT],
          t["SAB"][:, 2 * QT:3 * QT], t["SAB"][:, 3 * QT:4 * QT]]
    gp_state = {"i": 0, "free": [0] * len(GP)}
    wslot_cons = [0, 0]

    def gp_next():
        g = gp_state["i"] % len(GP)
        gp_state["i"] += 1
        return g

    def gp_mm(g, lhsT_rhs_list, out=None):
        tensor.wait_ge(s_ve.h, gp_state["free"][g])
        dst = GP[g] if out is None else out
        n = len(lhsT_rhs_list)
        for j, (lh, rh) in enumerate(lhsT_rhs_list):
            i = tensor.matmul(dst, lh, rh, start=(j == 0), stop=(j == n - 1))
        return s_mm.inc(i)

    def gp_evac(g, cp, dest, src=None):
        vector.wait_ge(s_mm.h, cp)
        i = vector.tensor_copy(dest, GP[g] if src is None else src)
        gp_state["free"][g] = s_ve.inc(i)
        return gp_state["free"][g]

    iosc = t["outst"][:, 0, 0:896]
    murow = t["outst"][0:1, 1:3, :].rearrange("p a b -> p (a b)")

    i = gpsimd.iota(iosc, [[1, 896]], channel_multiplier=-1,
                    allow_small_or_imprecise_dtypes=True)
    gpsimd.memset(t["onesf"][:, :], 1.0)
    gpsimd.memset(t["onesn"][:, :], 1.0)
    i2 = gpsimd.memset(t["onesb"][:, :], 1.0)
    s_w.inc(i2)
    cp_iota = s_w.n

    def wload(dst_ap, src_ap):
        gpsimd.dma_start(out=dst_ap, in_=src_ap).then_inc(s_w.h, 16)
        s_w.n += 16
        return s_w.n

    wload(t["wq"][:, :, :], _w3(t["wqh"]))
    wload(t["wk_t"], _w3(t["wkh"]))
    cp_w_kv = wload(t["wv_t"], _w3(t["wvh"]))
    gpsimd.dma_start(out=murow[0:1, 0:D], in_=t["mu"][0:1, :]).then_inc(s_w.h, 16)
    s_w.n += 16
    gpsimd.dma_start(out=murow[0:1, D:2 * D], in_=t["mu"][1:2, :]).then_inc(s_w.h, 16)
    s_w.n += 16
    cp_mu_dma = s_w.n

    vector.wait_ge(s_w.h, cp_iota)
    vector.tensor_scalar(t["msk"][:, :], iosc, 384.0, None, ALU.is_ge)
    vector.tensor_scalar(t["idf"][:, :], iosc[:, 0:128], 0.0, None, ALU.is_equal)
    vector.tensor_scalar(t["idb"][:, :], iosc[:, 0:128], 0.0, None, ALU.is_equal)
    for slot in range(2):
        vector.memset(t["vsup"][:, slot, :, 64:65], 1.0)
        vector.memset(t["vsup"][:, slot, :, 129:130], 1.0)
    tensor.wait_ge(s_w.h, cp_mu_dma)
    for v in range(2):
        off = v * D
        for cc in range(8):
            g = gp_next()
            cp = gp_mm(g, [(murow[0:1, off + cc * 128: off + (cc + 1) * 128],
                            t["onesf"][0:1, 0:1])], out=GP[g][:, 0:1])
            gp_evac(g, cp, t["muT"][:, v, cc:cc + 1], src=GP[g][:, 0:1])
    vector.tensor_scalar(t["muT"][:, 2, :], t["muT"][:, 0, :], -1.0, 1.0, ALU.mult, ALU.add)
    i = vector.tensor_scalar(t["muT"][:, 3, :], t["muT"][:, 1, :], -1.0, 1.0, ALU.mult, ALU.add)
    cp_mu_ve = s_ve.inc(i)

    sqjunk = t["outst"][:, 0, :]

    def ln_row(src1d, dest_bf):
        i = vector.reduce_sum(t["st6"][0:1, 0:1], src1d[0:1, :], axis=AX.X)
        s_ve.inc(i)
        scalar.wait_ge(s_ve.h, s_ve.n)
        i = scalar.activation(sqjunk[0:1, :], src1d[0:1, :], AF.Square,
                              accum_out=t["st6"][0:1, 1:2])
        s_ac.inc(i)
        vector.wait_ge(s_ac.h, s_ac.n)
        vector.tensor_scalar(t["mv2"][0:1, 0:1], t["st6"][0:1, 0:1], 1.0 / 1024.0, None, ALU.mult)
        vector.tensor_mul(t["st6"][0:1, 2:3], t["mv2"][0:1, 0:1], t["mv2"][0:1, 0:1])
        vector.tensor_scalar(t["mv2"][0:1, 1:2], t["st6"][0:1, 1:2], 1.0 / 1024.0, None, ALU.mult)
        vector.tensor_tensor(t["mv2"][0:1, 1:2], t["mv2"][0:1, 1:2], t["st6"][0:1, 2:3], ALU.subtract)
        i = vector.tensor_scalar_add(t["mv2"][0:1, 1:2], t["mv2"][0:1, 1:2], EPS)
        s_ve.inc(i)
        scalar.wait_ge(s_ve.h, s_ve.n)
        scalar.activation(t["rstd"][0:1, :], t["mv2"][0:1, 1:2], AF.Ln)
        i = scalar.activation(t["rstd"][0:1, :], t["rstd"][0:1, :], AF.Exp, scale=-0.5)
        s_ac.inc(i)
        vector.wait_ge(s_ac.h, s_ac.n)
        i = vector.tensor_scalar(dest_bf[0:1, :], src1d[0:1, :], t["mv2"][0:1, 0:1],
                                 t["rstd"][0:1, 0:1], ALU.subtract, ALU.mult)
        return s_ve.inc(i)

    sync.wait_ge(s_ve.h, cp_mu_ve)
    sync.dma_start(out=murow[0:1, 0:D], in_=t["xbound"][0:1, :]).then_inc(s_x.h, 16)
    s_x.n += 16
    sync.dma_start(out=murow[0:1, D:2 * D], in_=t["xbound"][1:2, :]).then_inc(s_x.h, 16)
    s_x.n += 16
    vector.wait_ge(s_x.h, s_x.n)
    ln_row(murow[0:1, 0:D], t["xnb0"])
    cp_b = ln_row(murow[0:1, D:2 * D], t["xnb1"])
    tensor.wait_ge(s_ve.h, cp_b)
    for ci, xnb in ((0, t["xnb0"]), (1, t["xnb1"])):
        for cc in range(8):
            g = gp_next()
            cp = gp_mm(g, [(xnb[0:1, cc * 128:(cc + 1) * 128], t["onesb"][0:1, 0:1])],
                       out=GP[g][:, 0:1])
            gp_evac(g, cp, t["xn_own"][:, ci, cc, 0:1], src=GP[g][:, 0:1])

    tensor.wait_ge(s_w.h, cp_w_kv)
    stt = t["stt"]
    g2free = [0, 0]
    stage_free = {0: 0, 1: 0}
    store_gate = {"k": 0, "v": 0}
    slotB = {}
    apply_cp = {}
    stats_cp = {}
    xn_done = {}
    dma_cp = {}

    def xn_view(s):
        if s in own_subs:
            ci, toff = own_subs[s]
            return t["xn_own"][:, ci, :, 1 + toff: 1 + toff + 512]
        return t["xnring"][:, s % 2, :, 0:512]

    def emit_dma(s):
        for j in range(4):
            if s >= 1:
                sync.wait_ge(s_mm.h, slotB[(s - 1, j)])
            sync.dma_start(out=t["xtm"][:, j, :],
                           in_=t["xb"][s * 512 + j * 128: s * 512 + (j + 1) * 128, :]
                           ).then_inc(s_x.h, 16)
            s_x.n += 16
        dma_cp[s] = s_x.n

    def emit_stats(s):
        vector.wait_ge(s_x.h, dma_cp[s])
        for j in range(4):
            i = vector.reduce_sum(stt[:, j:j + 1], t["xtm"][:, j, :], axis=AX.X)
        cp_rd = s_ve.inc(i)
        scalar.wait_ge(s_x.h, dma_cp[s])
        for j in range(4):
            i = scalar.activation(sqjunk, t["xtm"][:, j, :], AF.Square,
                                  accum_out=stt[:, 4 + j:5 + j])
        cp_sq = s_ac.inc(i)
        vector.wait_ge(s_ac.h, cp_sq)
        vector.wait_ge(s_ve.h, cp_rd)
        i = vector.tensor_scalar(stt[:, 8:12], stt[:, 0:4], 1.0 / 1024.0, None, ALU.mult)
        vector.wait_ge(s_ve.h, s_ve.inc(i))
        i = vector.tensor_mul(stt[:, 12:16], stt[:, 8:12], stt[:, 8:12])
        s_ve.inc(i)
        i = vector.tensor_scalar(stt[:, 20:24], stt[:, 4:8], 1.0 / 1024.0, None, ALU.mult)
        vector.wait_ge(s_ve.h, s_ve.inc(i))
        i = vector.tensor_tensor(stt[:, 12:16], stt[:, 20:24], stt[:, 12:16], ALU.subtract)
        vector.wait_ge(s_ve.h, s_ve.inc(i))
        i = vector.tensor_scalar_add(stt[:, 12:16], stt[:, 12:16], EPS)
        cp_v = s_ve.inc(i)
        scalar.wait_ge(s_ve.h, cp_v)
        scalar.activation(stt[:, 16:20], stt[:, 12:16], AF.Ln)
        i = scalar.activation(stt[:, 16:20], stt[:, 16:20], AF.Exp, scale=-0.5)
        stats_cp[s] = s_ac.inc(i)

    def emit_apply(s, tt):
        slot = tt % 2
        stage = t["ebufAB"][:, slot, :]
        vector.wait_ge(s_ac.h, stats_cp[s])
        vector.wait_ge(s_mm.h, stage_free[slot])
        i = vector.tensor_scalar(stage, t["xtm"][:, tt, :], stt[:, 8 + tt:9 + tt],
                                 stt[:, 16 + tt:17 + tt], ALU.subtract, ALU.mult)
        apply_cp[(s, tt)] = s_ve.inc(i)

    def emit_T(s, tt):
        slot = tt % 2
        stage = t["ebufAB"][:, slot, :]
        dv = xn_view(s)[:, :, tt * 128:(tt + 1) * 128]
        tensor.wait_ge(s_ve.h, max(apply_cp[(s, tt)], g2free[0]))
        for j in range(8):
            i = tensor.transpose(t["G2"][:, j * 128:(j + 1) * 128],
                                 stage[:, j * 128:(j + 1) * 128], t["idb"][:, :])
        cpB = s_mm.inc(i)
        slotB[(s, tt)] = cpB
        stage_free[slot] = cpB
        vector.wait_ge(s_mm.h, cpB)
        i = vector.tensor_copy(dv[:, :, :],
                               t["G2"][:, :].rearrange("p (c f) -> p c f", c=8, f=128))
        g2free[0] = s_ve.inc(i)
        if tt == 3:
            xn_done[s] = g2free[0]
        if tt == 0:
            emit_apply(s, 2)
        elif tt == 1:
            emit_apply(s, 3)

    def emit_kv_group(sI, g):
        xnsrc = xn_view(sI)
        if g == 0:
            tensor.wait_ge(s_ve.h, xn_done[sI])
        if g < 8:
            fj = g
            gidx = gp_next()
            cp = gp_mm(gidx, [(t["wk_t"][:, cc, fj * 128:(fj + 1) * 128], xnsrc[:, cc, :])
                              for cc in range(8)])
            if g == 0:
                vector.wait_ge(s_kw.h, store_gate["k"])
            cpk = gp_evac(gidx, cp, t["kst"][:, fj, :])
            if g == 7:
                gpsimd.wait_ge(s_ve.h, cpk)
                gpsimd.dma_start(
                    out=t["kvk"].rearrange("(c p) t -> p c t", p=128)[:, :, sI * 512:(sI + 1) * 512],
                    in_=t["kst"][:, :, :]).then_inc(s_kw.h, 16)
                s_kw.n += 16
                store_gate["k"] = s_kw.n
        else:
            tj, dh = (g - 8) // 2, (g - 8) % 2
            gidx = gp_next()
            cp = gp_mm(gidx, [(xnsrc[:, cc, tj * 128:(tj + 1) * 128],
                               t["wv_t"][:, cc, dh * 512:(dh + 1) * 512])
                              for cc in range(8)])
            if g == 8:
                vector.wait_ge(s_kw.h, store_gate["v"])
            cpv = gp_evac(gidx, cp, t["vst"][:, tj * 2 + dh, :])
            if g == 15:
                gpsimd.wait_ge(s_ve.h, cpv)
                gpsimd.dma_start(
                    out=t["kvv"][sI * 512:(sI + 1) * 512, :].rearrange(
                        "(j p) (h f) -> p j h f", p=128, h=2),
                    in_=t["vst"][:, :, :].rearrange("p (j h) f -> p j h f", h=2)
                ).then_inc(s_kw.h, 16)
                s_kw.n += 16
                store_gate["v"] = s_kw.n

    emit_dma(0)
    emit_stats(0)
    emit_apply(0, 0)
    emit_apply(0, 1)
    for it in range(NSUB + 1):
        sT, sKV = it, it - 1
        if sKV < 0:
            for tt in range(4):
                emit_T(sT, tt)
        elif sT < NSUB:
            for g in range(16):
                emit_kv_group(sKV, g)
                if g in (3, 6, 9, 12):
                    emit_T(sT, (3, 6, 9, 12).index(g))
        else:
            for g in range(16):
                emit_kv_group(sKV, g)
        if sT + 1 <= NSUB - 1:
            emit_dma(sT + 1)
            emit_stats(sT + 1)
            emit_apply(sT + 1, 0)
            emit_apply(sT + 1, 1)

    if os.environ.get("MKDBG"):
        sync.wait_ge(s_ve.h, xn_done[NSUB - 1])
        sync.dma_start(out=t["dbg_xn"][:, :],
                       in_=t["xn_own"][:, :, :, :].rearrange("p a b c -> p (a b c)")
                       ).then_inc(s_out.h, 16)
        s_out.n += 16
        sync.dma_start(out=t["dbg_muT"][:, :],
                       in_=t["muT"][:, :, :].rearrange("p a b -> p (a b)")).then_inc(s_out.h, 16)
        s_out.n += 16

    xslot_free = [0] * 4
    cp_ph2_mm = s_mm.n
    gpsimd.wait_ge(s_mm.h, cp_ph2_mm)
    cp_wvt_res = wload(t["wv_t"], _w3(t["wvt"]))
    mix_gate = [cp_ph2_mm]
    cp_kv_done = s_kw.n
    sync.wait_ge(s_kw.h, cp_kv_done)
    if os.environ.get("MKDBG"):
        sync.dma_start(out=t["dbg_kvk"][:, :], in_=t["kvk"][:, 0:1024]).then_inc(s_out.h, 16)
        s_out.n += 16
        sync.dma_start(out=t["dbg_kvv"][:, :], in_=t["kvv"][0:1024, :]).then_inc(s_out.h, 16)
        s_out.n += 16

    norm_state = {"ve": 0, "sh": s_kw.n, "bc": 0, "pend_rc": None, "pend": None}
    sup = {"last": [0, 0], "cp": {}, "base": {}}
    _run = 0
    for q_ in range(4):
        _ch = (chunkA if q_ < 2 else chunkB)
        _ns = ((_ch * CH + (q_ % 2) * 512) // KC + 4 + SUP - 1) // SUP
        for h_ in range(NHP):
            sup["base"][(q_, h_)] = _run % 2
            _run += _ns

    def issue_sup_g(q_, h_, ks):
        slot = (sup["base"][(q_, h_)] + ks) % 2
        sync.wait_ge(s_mm.h, sup["last"][slot])
        sync.dma_start(out=t["ksup"][:, slot, :],
                       in_=t["kvk"][h_ * 128:(h_ + 1) * 128,
                                    ks * 1024:(ks + 1) * 1024]).then_inc(s_kr.h, 16)
        s_kr.n += 16
        for b_ in range(2):
            sync.dma_start(
                out=t["vsup"][:, slot, :, b_ * 65: b_ * 65 + 64],
                in_=t["kvv"][ks * 1024:(ks + 1) * 1024,
                             h_ * 128 + b_ * 64: h_ * 128 + b_ * 64 + 64]
                    .rearrange("(a p) f -> p a f", p=128)).then_inc(s_kr.h, 16)
            s_kr.n += 16
        sup["cp"][(q_, h_, ks)] = s_kr.n

    for qt in range(4):
        qt_out_gate = s_out.n
        ci = 0 if qt < 2 else 1
        chunk = chunkA if qt < 2 else chunkB
        toff = (qt % 2) * 512
        q0 = chunk * CH + toff
        kc_end = q0 // KC + 4
        nsup = (kc_end + SUP - 1) // SUP
        xcur = t["xn_own"][:, ci, :, 1 + toff: 1 + toff + 512]
        xprev = t["xn_own"][:, ci, :, toff: toff + 512]

        for fj in range(8):
            g = gp_next()
            cp = gp_mm(g, [(t["wq"][:, cc, fj * 128:(fj + 1) * 128], xcur[:, cc, :])
                           for cc in range(8)])
            cp_q = gp_evac(g, cp, t["qcol"][:, fj, :])

        if os.environ.get("MKDBG") and qt == 0:
            sync.wait_ge(s_ve.h, cp_q)
            sync.dma_start(out=t["dbg_q"][:, :],
                           in_=t["qcol"][:, :, :].rearrange("p a b -> p (a b)")).then_inc(s_out.h, 16)
            s_out.n += 16

        vector.wait_ge(s_mm.h, mix_gate[0])
        for cc in range(8):
            vector.tensor_scalar(t["xr_t"][:, cc, :], xcur[:, cc, :],
                                 t["muT"][:, 0, cc:cc + 1], None, ALU.mult)
            vector.scalar_tensor_tensor(t["xr_t"][:, cc, :], xprev[:, cc, :],
                                        t["muT"][:, 2, cc:cc + 1], t["xr_t"][:, cc, :],
                                        ALU.mult, ALU.add)
            vector.tensor_scalar(t["xk_t"][:, cc, :], xcur[:, cc, :],
                                 t["muT"][:, 1, cc:cc + 1], None, ALU.mult)
            i = vector.scalar_tensor_tensor(t["xk_t"][:, cc, :], xprev[:, cc, :],
                                            t["muT"][:, 3, cc:cc + 1], t["xk_t"][:, cc, :],
                                            ALU.mult, ALU.add)
        cp_mix = s_ve.inc(i)
        gpsimd.wait_ge(s_mm.h, wslot_cons[0])
        cp_wrt = wload(t["wslot"][:, 0, :, :], _w3(t["wrt"]))
        gpsimd.wait_ge(s_mm.h, wslot_cons[1])
        cp_wkt = wload(t["wslot"][:, 1, :, :], _w3(t["wkt"]))

        for hp in range(NHP):
            sc_cp = {}
            av_cp = {}
            e_cp = {}
            sbase = sup["base"][(qt, hp)]

            if (qt, hp, 0) not in sup["cp"]:
                issue_sup_g(qt, hp, 0)
            if nsup > 1 and (qt, hp, 1) not in sup["cp"]:
                issue_sup_g(qt, hp, 1)
            tensor.wait_ge(s_ve.h, cp_q)

            def em_scores(kc):
                half = kc % 2
                ks = kc // SUP
                tensor.wait_ge(s_kr.h, sup["cp"][(qt, hp, ks)])
                if kc == 2 and norm_state["pend"]:
                    tensor.wait_ge(s_ve.h, norm_state["pend"][1])
                if kc >= 2:
                    tensor.wait_ge(s_ac.h, e_cp[kc - 2][2])
                slot = (sbase + ks) % 2
                tensor.matmul(t["SAB"][:, half * 2 * QT: half * 2 * QT + QT],
                              t["ksup"][0:64, slot, (kc % SUP) * KC:(kc % SUP + 1) * KC],
                              t["qcol"][0:64, hp, :], start=True, stop=True)
                i = tensor.matmul(t["SAB"][:, half * 2 * QT + QT: (half + 1) * 2 * QT],
                                  t["ksup"][64:128, slot, (kc % SUP) * KC:(kc % SUP + 1) * KC],
                                  t["qcol"][64:128, hp, :], start=True, stop=True)
                sc_cp[kc] = s_mm.inc(i)

            def em_exp(kc):
                half = kc % 2
                eh = kc % 4
                scalar.wait_ge(s_mm.h, sc_cp[kc])
                i = scalar.activation(t["ebufAB"][:, eh, :],
                                      t["SAB"][:, half * 2 * QT:(half + 1) * 2 * QT],
                                      AF.Exp, scale=SCALE)
                ac = s_ac.inc(i)
                c = kc * KC - q0
                if 0 <= c < 512:
                    vector.wait_ge(s_ac.h, ac)
                    m = t["msk"][:, 384 - c: 896 - c]
                    vector.tensor_tensor(t["ebufAB"][:, eh, 0:QT],
                                         t["ebufAB"][:, eh, 0:QT], m, ALU.mult)
                    i = vector.tensor_tensor(t["ebufAB"][:, eh, QT:2 * QT],
                                             t["ebufAB"][:, eh, QT:2 * QT], m, ALU.mult)
                    e_cp[kc] = ("ve", s_ve.inc(i), ac)
                else:
                    e_cp[kc] = ("ac", ac, ac)

            def em_av(kc):
                eh = kc % 4
                kind, cpv, _ = e_cp[kc]
                tensor.wait_ge(s_ac.h if kind == "ac" else s_ve.h, cpv)
                ks = kc // SUP
                slot = (sbase + ks) % 2
                vk = t["vsup"][:, slot, kc % SUP, :]
                tensor.matmul(t["AVa"][0:65, :], vk[:, 0:65], t["ebufAB"][:, eh, 0:QT],
                              start=(kc == 0), stop=(kc == kc_end - 1))
                i = tensor.matmul(t["AVb"][0:65, :], vk[:, 65:130], t["ebufAB"][:, eh, QT:2 * QT],
                                  start=(kc == 0), stop=(kc == kc_end - 1))
                av_cp[kc] = s_mm.inc(i)
                sup["last"][slot] = av_cp[kc]
                if (kc + 1) % SUP == 0:
                    ks_next = kc // SUP + 2
                    if ks_next < nsup:
                        issue_sup_g(qt, hp, ks_next)

            em_scores(0)
            em_exp(0)
            em_scores(1)
            em_exp(1)
            def emit_norm_tail():
                if norm_state["pend_rc"] is None:
                    return
                php, cp_rc = norm_state["pend_rc"]
                norm_state["pend_rc"] = None
                tensor.wait_ge(s_ve.h, cp_rc)
                tensor.matmul(t["G1"][0:64, :], t["onesn"][64:65, :],
                              t["rb"][64:65, 0:512], start=True, stop=True)
                i = tensor.matmul(t["SAB"][0:64, 0:QT], t["onesn"][64:65, :],
                                  t["rb"][64:65, 512:1024], start=True, stop=True)
                cp_bc = s_mm.inc(i)
                vector.wait_ge(s_mm.h, cp_bc)
                i = vector.tensor_tensor(t["attncol"][0:64, php, :], t["oa"][0:64, :],
                                         t["G1"][0:64, :], ALU.mult)
                cp_mA = s_ve.inc(i)
                vector.wait_ge(s_kw.h, norm_state["sh"])
                i = vector.tensor_tensor(t["atm"][0:64, :], t["aev"][0:64, :],
                                         t["SAB"][0:64, 0:QT], ALU.mult)
                cp_mB = s_ve.inc(i)
                gpsimd.wait_ge(s_ve.h, cp_mB)
                gpsimd.dma_start(out=t["attncol"][64:128, php, :],
                                 in_=t["atm"][0:64, :]).then_inc(s_kw.h, 16)
                s_kw.n += 16
                norm_state["sh"] = s_kw.n
                norm_state["bc"] = cp_bc
                norm_state["pend"] = (True, cp_mB)
                gp_state["free"][0] = cp_mA
                gp_state["free"][3] = cp_mB

            av_gate_done = False
            for kc in range(kc_end):
                if not av_gate_done:
                    tensor.wait_ge(s_ve.h, norm_state["ve"])
                    av_gate_done = True
                em_av(kc)
                if kc == 0:
                    emit_norm_tail()
                if kc + 2 < kc_end:
                    em_scores(kc + 2)
                    em_exp(kc + 2)

            vector.wait_ge(s_mm.h, av_cp[kc_end - 1])
            if hp == 0:
                vector.wait_ge(s_out.h, qt_out_gate)
            vector.reciprocal(t["rb"][64:65, 0:512], t["AVa"][64:65, :])
            i = vector.reciprocal(t["rb"][64:65, 512:1024], t["AVb"][64:65, :])
            norm_state["pend_rc"] = (hp, s_ve.inc(i))
            vector.tensor_copy(t["oa"][0:65, :], t["AVa"][0:65, :])
            i = vector.tensor_copy(t["aev"][0:65, :], t["AVb"][0:65, :])
            cp_ev = s_ve.inc(i)
            norm_state["ve"] = cp_ev
            gp_state["free"][1] = cp_ev
            gp_state["free"][2] = cp_ev
            if hp + 1 < NHP:
                issue_sup_g(qt, hp + 1, 0)
            elif qt + 1 < 4:
                issue_sup_g(qt + 1, 0, 0)
        emit_norm_tail()
        cp_attn = s_ve.n
        if os.environ.get("MKDBG") and qt == 0:
            sync.wait_ge(s_ve.h, cp_attn)
            sync.dma_start(out=t["dbg_att"][:, :],
                           in_=t["attncol"][:, :, :].rearrange("p a b -> p (a b)")).then_inc(s_out.h, 16)
            s_out.n += 16
            sync.dma_start(out=t["dbg_e"][:, :],
                           in_=t["ebufAB"][:, 0, :]).then_inc(s_out.h, 16)
            s_out.n += 16

        tensor.wait_ge(s_ve.h, cp_mix)
        tensor.wait_ge(s_w.h, cp_wrt)
        for dj in range(8):
            g = gp_next()
            cp = gp_mm(g, [(t["wslot"][:, 0, cc, dj * 128:(dj + 1) * 128], t["xr_t"][:, cc, :])
                           for cc in range(8)])
            scalar.wait_ge(s_mm.h, cp)
            i = scalar.activation(t["r_t"][:, dj, :], GP[g], AF.Sigmoid)
            ac = s_ac.inc(i)
            vector.wait_ge(s_ac.h, ac)
            i = vector.tensor_scalar(t["rcp"][:, 0:1], t["rcp"][:, 0:1], 1.0, None, ALU.mult)
            gp_state["free"][g] = s_ve.inc(i)
        cp_r_mm = s_mm.n

        gpsimd.wait_ge(s_mm.h, cp_r_mm)
        cp_wo = wload(t["wslot"][:, 0, :, :], _w3(t["wo"]))

        tensor.wait_ge(s_w.h, cp_wkt)
        for dj in range(8):
            g = gp_next()
            cp = gp_mm(g, [(t["wslot"][:, 1, cc, dj * 128:(dj + 1) * 128], t["xk_t"][:, cc, :])
                           for cc in range(8)])
            vector.wait_ge(s_mm.h, cp)
            vector.tensor_scalar_max(t["k1"][:, :], GP[g], 0.0)
            i = vector.tensor_mul(t["kact_t"][:, dj, :], t["k1"][:, :], t["k1"][:, :])
            gp_state["free"][g] = s_ve.inc(i)
        cp_k_mm = s_mm.n
        wslot_cons[1] = cp_k_mm
        mix_gate[0] = cp_k_mm

        tensor.wait_ge(s_w.h, cp_wvt_res)
        for dj in range(8):
            g = gp_next()
            cp = gp_mm(g, [(t["wv_t"][:, cc, dj * 128:(dj + 1) * 128], t["kact_t"][:, cc, :])
                           for cc in range(8)])
            vector.wait_ge(s_mm.h, cp)
            i = vector.tensor_tensor(t["ffncol"][:, dj, :], GP[g], t["r_t"][:, dj, :], ALU.mult)
            gp_state["free"][g] = s_ve.inc(i)

        if os.environ.get("MKDBG") and qt == 0:
            sync.wait_ge(s_ve.h, s_ve.n)
            sync.dma_start(out=t["dbg_ffn"][:, :],
                           in_=t["ffncol"][:, :, :].rearrange("p a b -> p (a b)")).then_inc(s_out.h, 16)
            s_out.n += 16

        sync.wait_ge(s_mm.h, norm_state["bc"])
        for j in range(4):
            sync.wait_ge(s_ve.h, xslot_free[j])
            sync.dma_start(out=t["xtm"][:, j, :],
                           in_=t["xb"][q0 + j * 128: q0 + (j + 1) * 128, :]).then_inc(s_x.h, 16)
            s_x.n += 16
        cp_xrr = s_x.n

        cp_out_prev = s_out.n
        vector.wait_ge(s_out.h, cp_out_prev)
        tensor.wait_ge(s_ve.h, cp_attn)
        tensor.wait_ge(s_kw.h, norm_state["sh"])
        tensor.wait_ge(s_w.h, cp_wo)
        for fo in range(8):
            g = gp_next()
            cp = gp_mm(g, [(t["wslot"][:, 0, cc, fo * 128:(fo + 1) * 128], t["attncol"][:, cc, :])
                           for cc in range(8)])
            vector.wait_ge(s_mm.h, cp)
            i = vector.tensor_tensor(t["k1"][:, :], GP[g], t["ffncol"][:, fo, :], ALU.add)
            gp_state["free"][g] = s_ve.inc(i)
            tensor.wait_ge(s_ve.h, s_ve.n)
            for j in range(4):
                i = tensor.transpose(t["G1"][:, j * 128:(j + 1) * 128],
                                     t["k1"][:, j * 128:(j + 1) * 128], t["idf"][:, :])
            cp2 = s_mm.inc(i)
            vector.wait_ge(s_mm.h, cp2)
            vector.wait_ge(s_x.h, cp_xrr)
            i = vector.tensor_tensor(
                t["outst"][:, :, fo * 128:(fo + 1) * 128],
                t["G1"][:, :].rearrange("p (j f) -> p j f", j=4),
                t["xtm"][:, :, fo * 128:(fo + 1) * 128], ALU.add)
            s_ve.inc(i)
            tensor.wait_ge(s_ve.h, s_ve.n)
        wslot_cons[0] = s_mm.n
        cp_comb = s_ve.n
        sync.wait_ge(s_ve.h, cp_comb)
        sync.dma_start(
            out=t["out"][qt * 512:(qt + 1) * 512, :].rearrange("(j p) f -> p j f", p=128),
            in_=t["outst"][:, :, :]).then_inc(s_out.h, 16)
        s_out.n += 16
        for j in range(4):
            xslot_free[j] = cp_comb

    sync.wait_ge(s_out.h, s_out.n)


_NC_CACHE = [None]


def kernel(L0_attn_x, L0_attn_w_qkv, L0_attn_w_o, L0_ffn_W_r, L0_ffn_W_k,
           L0_ffn_W_v, L0_ffn_mu_r, L0_ffn_mu_k):
    x = np.ascontiguousarray(np.asarray(L0_attn_x, dtype=np.float32))
    w_qkv = np.asarray(L0_attn_w_qkv, dtype=np.float32)
    NH, HD = 16, 64
    qkv3 = w_qkv.reshape(D, NH, 3 * HD)
    wqh = np.ascontiguousarray(qkv3[:, :, 0:HD].reshape(D, D))
    wkh = np.ascontiguousarray(qkv3[:, :, HD:2 * HD].reshape(D, D))
    wvh = np.ascontiguousarray(qkv3[:, :, 2 * HD:3 * HD].reshape(D, D))
    wrt = np.ascontiguousarray(np.asarray(L0_ffn_W_r, np.float32).T)
    wkt = np.ascontiguousarray(np.asarray(L0_ffn_W_k, np.float32).T)
    wvt = np.ascontiguousarray(np.asarray(L0_ffn_W_v, np.float32).T)
    wo_ = np.ascontiguousarray(np.asarray(L0_attn_w_o, np.float32))
    mu_ = np.ascontiguousarray(
        np.stack([np.asarray(L0_ffn_mu_r, np.float32), np.asarray(L0_ffn_mu_k, np.float32)]))

    if _NC_CACHE[0] is None:
        _NC_CACHE[0] = build()
    nc = _NC_CACHE[0]

    in_maps = []
    for i in range(8):
        b, p = i // 4, i % 4
        a0, b0 = p * CH, (7 - p) * CH
        xbound = np.zeros((2, D), np.float32)
        if a0 > 0:
            xbound[0] = x[b, a0 - 1]
        xbound[1] = x[b, b0 - 1]
        in_maps.append(dict(
            xb=x[b], xbound=xbound, wqh=wqh, wkh=wkh, wvh=wvh,
            wrt=wrt, wkt=wkt, wvt=wvt, wo=wo_, mu=mu_))
    res = run_bass_kernel_spmd(nc, in_maps, core_ids=list(range(8)))
    outp = np.empty((2, T, D), np.float32)
    for i in range(8):
        b, p = i // 4, i % 4
        o = res.results[i]["out"]
        outp[b, p * CH:(p + 1) * CH] = o[0:CH]
        outp[b, (7 - p) * CH:(8 - p) * CH] = o[CH:2 * CH]
    return outp


# version 28
# speedup vs baseline: 1.5975x; 1.0375x over previous
import os
import numpy as np
import concourse.bass as bass
import concourse.mybir as mybir
from concourse.bass_utils import run_bass_kernel_spmd

F32 = mybir.dt.float32
BF16 = mybir.dt.bfloat16
AF = mybir.ActivationFunctionType
ALU = mybir.AluOpType
AX = mybir.AxisListType

D = 1024
T = 8192
CH = 1024
QT = 512
KC = 128
SUP = 8
NHP = 8
EPS = 1e-5
SCALE = 1.0 / 32.0


class Sem:
    def __init__(self, ctx, nc, name):
        self.h = ctx.enter_context(nc.semaphore(name))
        self.n = 0

    def inc(self, instr, k=1):
        instr.then_inc(self.h, k)
        self.n += k
        return self.n


def _w3(src):
    return src.rearrange("(c p) f -> p c f", p=128)


def build():
    nc = bass.Bass("TRN2", target_bir_lowering=False)

    d_in = {}
    for name in ["wqh", "wkh", "wvh", "wrt", "wkt", "wvt", "wo"]:
        d_in[name] = nc.dram_tensor(name, [D, D], F32, kind="ExternalInput").ap()
    d_in["xb"] = nc.dram_tensor("xb", [T, D], F32, kind="ExternalInput").ap()
    d_in["xbound"] = nc.dram_tensor("xbound", [2, D], F32, kind="ExternalInput").ap()
    d_in["mu"] = nc.dram_tensor("mu", [2, D], F32, kind="ExternalInput").ap()
    d_in["out"] = nc.dram_tensor("out", [2 * CH, D], F32, kind="ExternalOutput").ap()
    d_in["kvk"] = nc.dram_tensor("kvk", [D, T], BF16, kind="Internal").ap()
    d_in["kvv"] = nc.dram_tensor("kvv", [T, D], BF16, kind="Internal").ap()
    if os.environ.get("MKDBG"):
        d_in["dbg_xn"] = nc.dram_tensor("dbg_xn", [128, 2 * 8 * 1026], BF16, kind="ExternalOutput").ap()
        d_in["dbg_muT"] = nc.dram_tensor("dbg_muT", [128, 32], F32, kind="ExternalOutput").ap()
        d_in["dbg_q"] = nc.dram_tensor("dbg_q", [128, 8 * 512], BF16, kind="ExternalOutput").ap()
        d_in["dbg_kvk"] = nc.dram_tensor("dbg_kvk", [D, 1024], BF16, kind="ExternalOutput").ap()
        d_in["dbg_kvv"] = nc.dram_tensor("dbg_kvv", [1024, D], BF16, kind="ExternalOutput").ap()
        d_in["dbg_att"] = nc.dram_tensor("dbg_att", [128, 8 * 512], BF16, kind="ExternalOutput").ap()
        d_in["dbg_ffn"] = nc.dram_tensor("dbg_ffn", [128, 8 * 512], BF16, kind="ExternalOutput").ap()
        d_in["dbg_e"] = nc.dram_tensor("dbg_e", [128, 2 * 512], BF16, kind="ExternalOutput").ap()
        d_in["dbg_oa"] = nc.dram_tensor("dbg_oa", [128, 512], F32, kind="ExternalOutput").ap()
        d_in["dbg_ob"] = nc.dram_tensor("dbg_ob", [128, 512], F32, kind="ExternalOutput").ap()
        d_in["dbg_atm"] = nc.dram_tensor("dbg_atm", [128, 512], BF16, kind="ExternalOutput").ap()

    from contextlib import ExitStack
    with ExitStack() as ctx:
        E_ = ctx.enter_context
        t = dict(d_in)

        t["xn_own"] = E_(nc.sbuf_tensor("xn_own", [128, 2, 8, 1026], BF16))
        t["wq"] = E_(nc.sbuf_tensor("wq", [128, 8, D], BF16))
        t["wslot"] = E_(nc.sbuf_tensor("wslot", [128, 2, 8, D], BF16))
        t["qcol"] = E_(nc.sbuf_tensor("qcol", [128, 8, QT], BF16))
        t["xtm"] = E_(nc.sbuf_tensor("xtm", [128, 4, D], F32))
        t["ksup"] = E_(nc.sbuf_tensor("ksup", [128, 2, SUP * KC], BF16))
        t["vsup"] = E_(nc.sbuf_tensor("vsup", [128, 2, SUP, 130], BF16))
        t["ebufAB"] = E_(nc.sbuf_tensor("ebufAB", [128, 4, 2 * QT], BF16))
        t["oa"] = E_(nc.sbuf_tensor("oa", [128, QT], F32))
        t["atm"] = E_(nc.sbuf_tensor("atm", [128, 512], BF16))
        t["onesn"] = E_(nc.sbuf_tensor("onesn", [128, 64], F32))
        t["outst"] = E_(nc.sbuf_tensor("outst", [128, 4, D], F32))
        t["msk"] = E_(nc.sbuf_tensor("msk", [128, 896], BF16))
        t["idf"] = E_(nc.sbuf_tensor("idf", [128, 128], F32))
        t["idb"] = E_(nc.sbuf_tensor("idb", [128, 128], BF16))
        t["muT"] = E_(nc.sbuf_tensor("muT", [128, 4, 8], F32))
        t["st6"] = E_(nc.sbuf_tensor("st6", [128, 12], F32))
        t["stt"] = E_(nc.sbuf_tensor("stt", [128, 24], F32))
        t["mv2"] = E_(nc.sbuf_tensor("mv2", [128, 2], F32))
        t["rstd"] = E_(nc.sbuf_tensor("rstd", [128, 1], F32))
        t["rcp"] = E_(nc.sbuf_tensor("rcp", [128, 1], F32))
        t["onesf"] = E_(nc.sbuf_tensor("onesf", [1, 2], F32))
        t["onesb"] = E_(nc.sbuf_tensor("onesb", [1, 2], BF16))
        ar1 = E_(nc.sbuf_tensor("ar1", [128, 8192], BF16))
        ar2 = E_(nc.sbuf_tensor("ar2", [128, 16384], BF16))
        ar3 = E_(nc.sbuf_tensor("ar3", [128, 8192], BF16))

        t["xnring"] = ar1[:, :].rearrange("p (s c t) -> p s c t", s=2, c=8, t=512)
        t["xr_t"] = ar1[:, 0:4096].rearrange("p (c t) -> p c t", c=8, t=512)
        t["xk_t"] = ar1[:, 4096:8192].rearrange("p (c t) -> p c t", c=8, t=512)
        t["wk_t"] = ar2[:, 0:8192].rearrange("p (c f) -> p c f", c=8, f=1024)
        t["wv_t"] = ar2[:, 8192:16384].rearrange("p (c f) -> p c f", c=8, f=1024)
        t["r_t"] = ar2[:, 0:4096].rearrange("p (c t) -> p c t", c=8, t=512)
        t["kact_t"] = ar2[:, 4096:8192].rearrange("p (c t) -> p c t", c=8, t=512)
        t["kst"] = ar3[:, 0:4096].rearrange("p (c t) -> p c t", c=8, t=512)
        t["vst"] = ar3[:, 4096:8192].rearrange("p (c t) -> p c t", c=8, t=512)
        t["attncol"] = ar3[:, 0:4096].rearrange("p (c t) -> p c t", c=8, t=512)
        t["ffncol"] = ar3[:, 4096:8192].rearrange("p (c t) -> p c t", c=8, t=512)
        t["k1"] = t["oa"][:, :]
        t["rb"] = t["xtm"][:, 0, :]
        t["aev"] = t["outst"][:, 0, 0:QT]
        t["xnb0"] = ar3[0:1, 0:D]
        t["xnb1"] = ar3[0:1, D:2 * D]
        t["xbf"] = None

        t["SAB"] = E_(nc.psum_tensor("SAB", [128, 4 * QT], F32))
        t["AVa"] = E_(nc.psum_tensor("AVa", [128, QT], F32))
        t["AVb"] = E_(nc.psum_tensor("AVb", [128, QT], F32))
        t["G1"] = E_(nc.psum_tensor("G1", [128, QT], F32))
        t["G2"] = E_(nc.psum_tensor("G2", [128, 1024], BF16))

        sems = {k: Sem(ctx, nc, k) for k in
                ["s_x", "s_w", "s_kw", "s_kr", "s_mm", "s_ve", "s_ac", "s_out"]}

        engines = [nc.sync, nc.tensor, nc.vector, nc.scalar, nc.gpsimd]
        pid = nc.partition_id([e.engine for e in engines])
        idx = [pid % 4 for _ in engines]
        for p in nc.Switch(engines=engines, index=idx, n=4):
            emit_body(nc, p, sems, t)
    return nc


def emit_body(nc, p, sems, t):
    for s in sems.values():
        s.n = 0
    s_x, s_w, s_kw, s_kr = sems["s_x"], sems["s_w"], sems["s_kw"], sems["s_kr"]
    s_mm, s_ve, s_ac, s_out = sems["s_mm"], sems["s_ve"], sems["s_ac"], sems["s_out"]
    sync, tensor, vector, scalar, gpsimd = nc.sync, nc.tensor, nc.vector, nc.scalar, nc.gpsimd

    chunkA, chunkB = p, 7 - p
    PC = 8 - p
    NSUB = 2 * PC
    own_subs = {2 * chunkA: (0, 0), 2 * chunkA + 1: (0, 512),
                2 * chunkB: (1, 0), 2 * chunkB + 1: (1, 512)}

    GP = [t["G1"][:, :], t["AVa"][:, :], t["AVb"][:, :],
          t["SAB"][:, 0:QT], t["SAB"][:, QT:2 * QT],
          t["SAB"][:, 2 * QT:3 * QT], t["SAB"][:, 3 * QT:4 * QT]]
    gp_state = {"i": 0, "free": [0] * len(GP)}
    wslot_cons = [0, 0]

    def gp_next():
        g = gp_state["i"] % len(GP)
        gp_state["i"] += 1
        return g

    def gp_mm(g, lhsT_rhs_list, out=None):
        tensor.wait_ge(s_ve.h, gp_state["free"][g])
        dst = GP[g] if out is None else out
        n = len(lhsT_rhs_list)
        for j, (lh, rh) in enumerate(lhsT_rhs_list):
            i = tensor.matmul(dst, lh, rh, start=(j == 0), stop=(j == n - 1))
        return s_mm.inc(i)

    def gp_evac(g, cp, dest, src=None):
        vector.wait_ge(s_mm.h, cp)
        i = vector.tensor_copy(dest, GP[g] if src is None else src)
        gp_state["free"][g] = s_ve.inc(i)
        return gp_state["free"][g]

    iosc = t["outst"][:, 0, 0:896]
    murow = t["outst"][0:1, 1:3, :].rearrange("p a b -> p (a b)")

    i = gpsimd.iota(iosc, [[1, 896]], channel_multiplier=-1,
                    allow_small_or_imprecise_dtypes=True)
    gpsimd.memset(t["onesf"][:, :], 1.0)
    gpsimd.memset(t["onesn"][:, :], 1.0)
    i2 = gpsimd.memset(t["onesb"][:, :], 1.0)
    s_w.inc(i2)
    cp_iota = s_w.n

    def wload(dst_ap, src_ap):
        gpsimd.dma_start(out=dst_ap, in_=src_ap).then_inc(s_w.h, 16)
        s_w.n += 16
        return s_w.n

    wload(t["wq"][:, :, :], _w3(t["wqh"]))
    wload(t["wk_t"], _w3(t["wkh"]))
    cp_w_kv = wload(t["wv_t"], _w3(t["wvh"]))
    gpsimd.dma_start(out=murow[0:1, 0:D], in_=t["mu"][0:1, :]).then_inc(s_w.h, 16)
    s_w.n += 16
    gpsimd.dma_start(out=murow[0:1, D:2 * D], in_=t["mu"][1:2, :]).then_inc(s_w.h, 16)
    s_w.n += 16
    cp_mu_dma = s_w.n

    vector.wait_ge(s_w.h, cp_iota)
    vector.tensor_scalar(t["msk"][:, :], iosc, 384.0, None, ALU.is_ge)
    vector.tensor_scalar(t["idf"][:, :], iosc[:, 0:128], 0.0, None, ALU.is_equal)
    vector.tensor_scalar(t["idb"][:, :], iosc[:, 0:128], 0.0, None, ALU.is_equal)
    for slot in range(2):
        vector.memset(t["vsup"][:, slot, :, 64:65], 1.0)
        vector.memset(t["vsup"][:, slot, :, 129:130], 1.0)
    tensor.wait_ge(s_w.h, cp_mu_dma)
    for v in range(2):
        off = v * D
        for cc in range(8):
            g = gp_next()
            cp = gp_mm(g, [(murow[0:1, off + cc * 128: off + (cc + 1) * 128],
                            t["onesf"][0:1, 0:1])], out=GP[g][:, 0:1])
            gp_evac(g, cp, t["muT"][:, v, cc:cc + 1], src=GP[g][:, 0:1])
    vector.tensor_scalar(t["muT"][:, 2, :], t["muT"][:, 0, :], -1.0, 1.0, ALU.mult, ALU.add)
    i = vector.tensor_scalar(t["muT"][:, 3, :], t["muT"][:, 1, :], -1.0, 1.0, ALU.mult, ALU.add)
    cp_mu_ve = s_ve.inc(i)

    sqjunk = t["outst"][:, 0, :]

    def ln_row(src1d, dest_bf):
        i = vector.reduce_sum(t["st6"][0:1, 0:1], src1d[0:1, :], axis=AX.X)
        s_ve.inc(i)
        scalar.wait_ge(s_ve.h, s_ve.n)
        i = scalar.activation(sqjunk[0:1, :], src1d[0:1, :], AF.Square,
                              accum_out=t["st6"][0:1, 1:2])
        s_ac.inc(i)
        vector.wait_ge(s_ac.h, s_ac.n)
        vector.tensor_scalar(t["mv2"][0:1, 0:1], t["st6"][0:1, 0:1], 1.0 / 1024.0, None, ALU.mult)
        vector.tensor_mul(t["st6"][0:1, 2:3], t["mv2"][0:1, 0:1], t["mv2"][0:1, 0:1])
        vector.tensor_scalar(t["mv2"][0:1, 1:2], t["st6"][0:1, 1:2], 1.0 / 1024.0, None, ALU.mult)
        vector.tensor_tensor(t["mv2"][0:1, 1:2], t["mv2"][0:1, 1:2], t["st6"][0:1, 2:3], ALU.subtract)
        i = vector.tensor_scalar_add(t["mv2"][0:1, 1:2], t["mv2"][0:1, 1:2], EPS)
        s_ve.inc(i)
        scalar.wait_ge(s_ve.h, s_ve.n)
        scalar.activation(t["rstd"][0:1, :], t["mv2"][0:1, 1:2], AF.Ln)
        i = scalar.activation(t["rstd"][0:1, :], t["rstd"][0:1, :], AF.Exp, scale=-0.5)
        s_ac.inc(i)
        vector.wait_ge(s_ac.h, s_ac.n)
        i = vector.tensor_scalar(dest_bf[0:1, :], src1d[0:1, :], t["mv2"][0:1, 0:1],
                                 t["rstd"][0:1, 0:1], ALU.subtract, ALU.mult)
        return s_ve.inc(i)

    sync.wait_ge(s_ve.h, cp_mu_ve)
    sync.dma_start(out=murow[0:1, 0:D], in_=t["xbound"][0:1, :]).then_inc(s_x.h, 16)
    s_x.n += 16
    sync.dma_start(out=murow[0:1, D:2 * D], in_=t["xbound"][1:2, :]).then_inc(s_x.h, 16)
    s_x.n += 16
    vector.wait_ge(s_x.h, s_x.n)
    ln_row(murow[0:1, 0:D], t["xnb0"])
    cp_b = ln_row(murow[0:1, D:2 * D], t["xnb1"])
    tensor.wait_ge(s_ve.h, cp_b)
    for ci, xnb in ((0, t["xnb0"]), (1, t["xnb1"])):
        for cc in range(8):
            g = gp_next()
            cp = gp_mm(g, [(xnb[0:1, cc * 128:(cc + 1) * 128], t["onesb"][0:1, 0:1])],
                       out=GP[g][:, 0:1])
            gp_evac(g, cp, t["xn_own"][:, ci, cc, 0:1], src=GP[g][:, 0:1])

    tensor.wait_ge(s_w.h, cp_w_kv)
    stt = t["stt"]
    g2free = [0, 0]
    stage_free = {0: 0, 1: 0}
    store_gate = {"k": 0, "v": 0}
    slotB = {}
    apply_cp = {}
    stats_cp = {}
    xn_done = {}
    dma_cp = {}

    def xn_view(s):
        if s in own_subs:
            ci, toff = own_subs[s]
            return t["xn_own"][:, ci, :, 1 + toff: 1 + toff + 512]
        return t["xnring"][:, s % 2, :, 0:512]

    def emit_dma(s):
        for j in range(4):
            if s >= 1:
                sync.wait_ge(s_mm.h, slotB[(s - 1, j)])
            sync.dma_start(out=t["xtm"][:, j, :],
                           in_=t["xb"][s * 512 + j * 128: s * 512 + (j + 1) * 128, :]
                           ).then_inc(s_x.h, 16)
            s_x.n += 16
        dma_cp[s] = s_x.n

    def emit_stats(s):
        vector.wait_ge(s_x.h, dma_cp[s])
        for j in range(4):
            i = vector.reduce_sum(stt[:, j:j + 1], t["xtm"][:, j, :], axis=AX.X)
        cp_rd = s_ve.inc(i)
        scalar.wait_ge(s_x.h, dma_cp[s])
        for j in range(4):
            i = scalar.activation(sqjunk, t["xtm"][:, j, :], AF.Square,
                                  accum_out=stt[:, 4 + j:5 + j])
        cp_sq = s_ac.inc(i)
        vector.wait_ge(s_ac.h, cp_sq)
        vector.wait_ge(s_ve.h, cp_rd)
        i = vector.tensor_scalar(stt[:, 8:12], stt[:, 0:4], 1.0 / 1024.0, None, ALU.mult)
        vector.wait_ge(s_ve.h, s_ve.inc(i))
        i = vector.tensor_mul(stt[:, 12:16], stt[:, 8:12], stt[:, 8:12])
        s_ve.inc(i)
        i = vector.tensor_scalar(stt[:, 20:24], stt[:, 4:8], 1.0 / 1024.0, None, ALU.mult)
        vector.wait_ge(s_ve.h, s_ve.inc(i))
        i = vector.tensor_tensor(stt[:, 12:16], stt[:, 20:24], stt[:, 12:16], ALU.subtract)
        vector.wait_ge(s_ve.h, s_ve.inc(i))
        i = vector.tensor_scalar_add(stt[:, 12:16], stt[:, 12:16], EPS)
        cp_v = s_ve.inc(i)
        scalar.wait_ge(s_ve.h, cp_v)
        scalar.activation(stt[:, 16:20], stt[:, 12:16], AF.Ln)
        i = scalar.activation(stt[:, 16:20], stt[:, 16:20], AF.Exp, scale=-0.5)
        stats_cp[s] = s_ac.inc(i)

    def emit_apply(s, tt):
        slot = tt % 2
        stage = t["ebufAB"][:, slot, :]
        vector.wait_ge(s_ac.h, stats_cp[s])
        vector.wait_ge(s_mm.h, stage_free[slot])
        i = vector.tensor_scalar(stage, t["xtm"][:, tt, :], stt[:, 8 + tt:9 + tt],
                                 stt[:, 16 + tt:17 + tt], ALU.subtract, ALU.mult)
        apply_cp[(s, tt)] = s_ve.inc(i)

    def emit_T(s, tt):
        slot = tt % 2
        stage = t["ebufAB"][:, slot, :]
        dv = xn_view(s)[:, :, tt * 128:(tt + 1) * 128]
        tensor.wait_ge(s_ve.h, max(apply_cp[(s, tt)], g2free[0]))
        for j in range(8):
            i = tensor.transpose(t["G2"][:, j * 128:(j + 1) * 128],
                                 stage[:, j * 128:(j + 1) * 128], t["idb"][:, :])
        cpB = s_mm.inc(i)
        slotB[(s, tt)] = cpB
        stage_free[slot] = cpB
        vector.wait_ge(s_mm.h, cpB)
        i = vector.tensor_copy(dv[:, :, :],
                               t["G2"][:, :].rearrange("p (c f) -> p c f", c=8, f=128))
        g2free[0] = s_ve.inc(i)
        if tt == 3:
            xn_done[s] = g2free[0]
        if tt == 0:
            emit_apply(s, 2)
        elif tt == 1:
            emit_apply(s, 3)

    def emit_kv_group(sI, g):
        xnsrc = xn_view(sI)
        if g == 0:
            tensor.wait_ge(s_ve.h, xn_done[sI])
        if g < 8:
            fj = g
            gidx = gp_next()
            cp = gp_mm(gidx, [(t["wk_t"][:, cc, fj * 128:(fj + 1) * 128], xnsrc[:, cc, :])
                              for cc in range(8)])
            if g == 0:
                vector.wait_ge(s_kw.h, store_gate["k"])
            cpk = gp_evac(gidx, cp, t["kst"][:, fj, :])
            if g == 7:
                gpsimd.wait_ge(s_ve.h, cpk)
                gpsimd.dma_start(
                    out=t["kvk"].rearrange("(c p) t -> p c t", p=128)[:, :, sI * 512:(sI + 1) * 512],
                    in_=t["kst"][:, :, :]).then_inc(s_kw.h, 16)
                s_kw.n += 16
                store_gate["k"] = s_kw.n
        else:
            tj, dh = (g - 8) // 2, (g - 8) % 2
            gidx = gp_next()
            cp = gp_mm(gidx, [(xnsrc[:, cc, tj * 128:(tj + 1) * 128],
                               t["wv_t"][:, cc, dh * 512:(dh + 1) * 512])
                              for cc in range(8)])
            if g == 8:
                vector.wait_ge(s_kw.h, store_gate["v"])
            cpv = gp_evac(gidx, cp, t["vst"][:, tj * 2 + dh, :])
            if g == 15:
                gpsimd.wait_ge(s_ve.h, cpv)
                gpsimd.dma_start(
                    out=t["kvv"][sI * 512:(sI + 1) * 512, :].rearrange(
                        "(j p) (h f) -> p j h f", p=128, h=2),
                    in_=t["vst"][:, :, :].rearrange("p (j h) f -> p j h f", h=2)
                ).then_inc(s_kw.h, 16)
                s_kw.n += 16
                store_gate["v"] = s_kw.n

    emit_dma(0)
    emit_stats(0)
    emit_apply(0, 0)
    emit_apply(0, 1)
    for it in range(NSUB + 1):
        sT, sKV = it, it - 1
        if sKV < 0:
            for tt in range(4):
                emit_T(sT, tt)
        elif sT < NSUB:
            for g in range(16):
                emit_kv_group(sKV, g)
                if g in (3, 6, 9, 12):
                    emit_T(sT, (3, 6, 9, 12).index(g))
        else:
            for g in range(16):
                emit_kv_group(sKV, g)
        if sT + 1 <= NSUB - 1:
            emit_dma(sT + 1)
            emit_stats(sT + 1)
            emit_apply(sT + 1, 0)
            emit_apply(sT + 1, 1)

    if os.environ.get("MKDBG"):
        sync.wait_ge(s_ve.h, xn_done[NSUB - 1])
        sync.dma_start(out=t["dbg_xn"][:, :],
                       in_=t["xn_own"][:, :, :, :].rearrange("p a b c -> p (a b c)")
                       ).then_inc(s_out.h, 16)
        s_out.n += 16
        sync.dma_start(out=t["dbg_muT"][:, :],
                       in_=t["muT"][:, :, :].rearrange("p a b -> p (a b)")).then_inc(s_out.h, 16)
        s_out.n += 16

    xslot_free = [0] * 4
    cp_ph2_mm = s_mm.n
    gpsimd.wait_ge(s_mm.h, cp_ph2_mm)
    cp_wvt_res = wload(t["wv_t"], _w3(t["wvt"]))
    mix_gate = [cp_ph2_mm]
    cp_kv_done = s_kw.n
    sync.wait_ge(s_kw.h, cp_kv_done)
    if os.environ.get("MKDBG"):
        sync.dma_start(out=t["dbg_kvk"][:, :], in_=t["kvk"][:, 0:1024]).then_inc(s_out.h, 16)
        s_out.n += 16
        sync.dma_start(out=t["dbg_kvv"][:, :], in_=t["kvv"][0:1024, :]).then_inc(s_out.h, 16)
        s_out.n += 16

    norm_state = {"ve": 0, "sh": s_kw.n, "bc": 0, "pend_rc": None, "pend": None,
                  "scrd": 0}
    sup = {"last": [0, 0], "cp": {}, "base": {}}
    _run = 0
    for q_ in range(4):
        _ch = (chunkA if q_ < 2 else chunkB)
        _ns = ((_ch * CH + (q_ % 2) * 512) // KC + 4 + SUP - 1) // SUP
        for h_ in range(NHP):
            sup["base"][(q_, h_)] = _run % 2
            _run += _ns

    def issue_sup_g(q_, h_, ks):
        slot = (sup["base"][(q_, h_)] + ks) % 2
        sync.wait_ge(s_mm.h, sup["last"][slot])
        sync.dma_start(out=t["ksup"][:, slot, :],
                       in_=t["kvk"][h_ * 128:(h_ + 1) * 128,
                                    ks * 1024:(ks + 1) * 1024]).then_inc(s_kr.h, 16)
        s_kr.n += 16
        for b_ in range(2):
            sync.dma_start(
                out=t["vsup"][:, slot, :, b_ * 65: b_ * 65 + 64],
                in_=t["kvv"][ks * 1024:(ks + 1) * 1024,
                             h_ * 128 + b_ * 64: h_ * 128 + b_ * 64 + 64]
                    .rearrange("(a p) f -> p a f", p=128)).then_inc(s_kr.h, 16)
            s_kr.n += 16
        sup["cp"][(q_, h_, ks)] = s_kr.n

    for qt in range(4):
        qt_out_gate = s_out.n
        ci = 0 if qt < 2 else 1
        chunk = chunkA if qt < 2 else chunkB
        toff = (qt % 2) * 512
        q0 = chunk * CH + toff
        kc_end = q0 // KC + 4
        nsup = (kc_end + SUP - 1) // SUP
        xcur = t["xn_own"][:, ci, :, 1 + toff: 1 + toff + 512]
        xprev = t["xn_own"][:, ci, :, toff: toff + 512]

        for fj in range(8):
            g = gp_next()
            cp = gp_mm(g, [(t["wq"][:, cc, fj * 128:(fj + 1) * 128], xcur[:, cc, :])
                           for cc in range(8)])
            cp_q = gp_evac(g, cp, t["qcol"][:, fj, :])

        if os.environ.get("MKDBG") and qt == 0:
            sync.wait_ge(s_ve.h, cp_q)
            sync.dma_start(out=t["dbg_q"][:, :],
                           in_=t["qcol"][:, :, :].rearrange("p a b -> p (a b)")).then_inc(s_out.h, 16)
            s_out.n += 16

        vector.wait_ge(s_mm.h, mix_gate[0])
        for cc in range(8):
            vector.tensor_scalar(t["xr_t"][:, cc, :], xcur[:, cc, :],
                                 t["muT"][:, 0, cc:cc + 1], None, ALU.mult)
            vector.scalar_tensor_tensor(t["xr_t"][:, cc, :], xprev[:, cc, :],
                                        t["muT"][:, 2, cc:cc + 1], t["xr_t"][:, cc, :],
                                        ALU.mult, ALU.add)
            vector.tensor_scalar(t["xk_t"][:, cc, :], xcur[:, cc, :],
                                 t["muT"][:, 1, cc:cc + 1], None, ALU.mult)
            i = vector.scalar_tensor_tensor(t["xk_t"][:, cc, :], xprev[:, cc, :],
                                            t["muT"][:, 3, cc:cc + 1], t["xk_t"][:, cc, :],
                                            ALU.mult, ALU.add)
        cp_mix = s_ve.inc(i)
        gpsimd.wait_ge(s_mm.h, wslot_cons[0])
        cp_wrt = wload(t["wslot"][:, 0, :, :], _w3(t["wrt"]))
        gpsimd.wait_ge(s_mm.h, wslot_cons[1])
        cp_wkt = wload(t["wslot"][:, 1, :, :], _w3(t["wkt"]))

        for hp in range(NHP):
            sc_cp = {}
            av_cp = {}
            e_cp = {}
            sbase = sup["base"][(qt, hp)]

            if (qt, hp, 0) not in sup["cp"]:
                issue_sup_g(qt, hp, 0)
            if nsup > 1 and (qt, hp, 1) not in sup["cp"]:
                issue_sup_g(qt, hp, 1)
            tensor.wait_ge(s_ve.h, cp_q)

            def em_scores(kc):
                half = kc % 2
                ks = kc // SUP
                tensor.wait_ge(s_kr.h, sup["cp"][(qt, hp, ks)])
                if kc == 2 and norm_state["pend"]:
                    tensor.wait_ge(s_ve.h, norm_state["pend"][1])
                if kc >= 2:
                    tensor.wait_ge(s_ac.h, e_cp[kc - 2][2])
                slot = (sbase + ks) % 2
                tensor.matmul(t["SAB"][:, half * 2 * QT: half * 2 * QT + QT],
                              t["ksup"][0:64, slot, (kc % SUP) * KC:(kc % SUP + 1) * KC],
                              t["qcol"][0:64, hp, :], start=True, stop=True)
                i = tensor.matmul(t["SAB"][:, half * 2 * QT + QT: (half + 1) * 2 * QT],
                                  t["ksup"][64:128, slot, (kc % SUP) * KC:(kc % SUP + 1) * KC],
                                  t["qcol"][64:128, hp, :], start=True, stop=True)
                sc_cp[kc] = s_mm.inc(i)

            def em_exp(kc):
                half = kc % 2
                eh = kc % 4
                scalar.wait_ge(s_mm.h, sc_cp[kc])
                i = scalar.activation(t["ebufAB"][:, eh, :],
                                      t["SAB"][:, half * 2 * QT:(half + 1) * 2 * QT],
                                      AF.Exp, scale=SCALE)
                ac = s_ac.inc(i)
                c = kc * KC - q0
                if 0 <= c < 512:
                    vector.wait_ge(s_ac.h, ac)
                    m = t["msk"][:, 384 - c: 896 - c]
                    vector.tensor_tensor(t["ebufAB"][:, eh, 0:QT],
                                         t["ebufAB"][:, eh, 0:QT], m, ALU.mult)
                    i = vector.tensor_tensor(t["ebufAB"][:, eh, QT:2 * QT],
                                             t["ebufAB"][:, eh, QT:2 * QT], m, ALU.mult)
                    e_cp[kc] = ("ve", s_ve.inc(i), ac)
                else:
                    e_cp[kc] = ("ac", ac, ac)

            def em_av(kc):
                eh = kc % 4
                kind, cpv, _ = e_cp[kc]
                tensor.wait_ge(s_ac.h if kind == "ac" else s_ve.h, cpv)
                ks = kc // SUP
                slot = (sbase + ks) % 2
                vk = t["vsup"][:, slot, kc % SUP, :]
                tensor.matmul(t["AVa"][0:65, :], vk[:, 0:65], t["ebufAB"][:, eh, 0:QT],
                              start=(kc == 0), stop=(kc == kc_end - 1))
                i = tensor.matmul(t["AVb"][0:65, :], vk[:, 65:130], t["ebufAB"][:, eh, QT:2 * QT],
                                  start=(kc == 0), stop=(kc == kc_end - 1))
                av_cp[kc] = s_mm.inc(i)
                sup["last"][slot] = av_cp[kc]
                if (kc + 1) % SUP == 0:
                    ks_next = kc // SUP + 2
                    if ks_next < nsup:
                        issue_sup_g(qt, hp, ks_next)

            em_scores(0)
            em_exp(0)
            em_scores(1)
            em_exp(1)
            def emit_norm_tail():
                if norm_state["pend_rc"] is None:
                    return
                php, rkind, cp_rc = norm_state["pend_rc"]
                norm_state["pend_rc"] = None
                tensor.wait_ge(s_ac.h if rkind == "ac" else s_ve.h, cp_rc)
                tensor.matmul(t["G1"][0:64, :], t["onesn"][64:65, :],
                              t["rb"][64:65, 0:512], start=True, stop=True)
                i = tensor.matmul(t["SAB"][0:64, 0:QT], t["onesn"][64:65, :],
                                  t["rb"][64:65, 512:1024], start=True, stop=True)
                cp_bc = s_mm.inc(i)
                vector.wait_ge(s_mm.h, cp_bc)
                i = vector.tensor_tensor(t["attncol"][0:64, php, :], t["oa"][0:64, :],
                                         t["G1"][0:64, :], ALU.mult)
                cp_mA = s_ve.inc(i)
                vector.wait_ge(s_kw.h, norm_state["sh"])
                i = vector.tensor_tensor(t["atm"][0:64, :], t["aev"][0:64, :],
                                         t["SAB"][0:64, 0:QT], ALU.mult)
                cp_mB = s_ve.inc(i)
                gpsimd.wait_ge(s_ve.h, cp_mB)
                gpsimd.dma_start(out=t["attncol"][64:128, php, :],
                                 in_=t["atm"][0:64, :]).then_inc(s_kw.h, 16)
                s_kw.n += 16
                norm_state["sh"] = s_kw.n
                norm_state["bc"] = cp_bc
                norm_state["pend"] = (True, cp_mB)
                gp_state["free"][0] = cp_mA
                gp_state["free"][3] = cp_mB

            av_gate_done = False
            for kc in range(kc_end):
                if not av_gate_done:
                    tensor.wait_ge(s_ve.h, norm_state["ve"])
                    if norm_state["scrd"]:
                        tensor.wait_ge(s_ac.h, norm_state["scrd"])
                    av_gate_done = True
                em_av(kc)
                if kc == 0:
                    emit_norm_tail()
                if kc + 2 < kc_end:
                    em_scores(kc + 2)
                    em_exp(kc + 2)

            vector.wait_ge(s_mm.h, av_cp[kc_end - 1])
            if hp == 0:
                vector.wait_ge(s_out.h, qt_out_gate)
            if kc_end <= 0:
                scalar.wait_ge(s_mm.h, av_cp[kc_end - 1])
                scalar.wait_ge(s_ve.h, xslot_free[0])
                scalar.activation(t["rb"][64:65, 0:512], t["AVa"][64:65, :], AF.Ln)
                scalar.activation(t["rb"][64:65, 0:512], t["rb"][64:65, 0:512],
                                  AF.Exp, scale=-1.0)
                scalar.activation(t["rb"][64:65, 512:1024], t["AVb"][64:65, :], AF.Ln)
                i = scalar.activation(t["rb"][64:65, 512:1024], t["rb"][64:65, 512:1024],
                                      AF.Exp, scale=-1.0)
                cp_rc = s_ac.inc(i)
                norm_state["scrd"] = cp_rc
                norm_state["pend_rc"] = (hp, "ac", cp_rc)
            else:
                vector.reciprocal(t["rb"][64:65, 0:512], t["AVa"][64:65, :])
                i = vector.reciprocal(t["rb"][64:65, 512:1024], t["AVb"][64:65, :])
                norm_state["pend_rc"] = (hp, "ve", s_ve.inc(i))
            vector.tensor_copy(t["oa"][0:65, :], t["AVa"][0:65, :])
            i = vector.tensor_copy(t["aev"][0:65, :], t["AVb"][0:65, :])
            cp_ev = s_ve.inc(i)
            norm_state["ve"] = cp_ev
            gp_state["free"][1] = cp_ev
            gp_state["free"][2] = cp_ev
            if hp + 1 < NHP:
                issue_sup_g(qt, hp + 1, 0)
            elif qt + 1 < 4:
                issue_sup_g(qt + 1, 0, 0)
        emit_norm_tail()
        cp_attn = s_ve.n
        if os.environ.get("MKDBG") and qt == 0:
            sync.wait_ge(s_ve.h, cp_attn)
            sync.dma_start(out=t["dbg_att"][:, :],
                           in_=t["attncol"][:, :, :].rearrange("p a b -> p (a b)")).then_inc(s_out.h, 16)
            s_out.n += 16
            sync.dma_start(out=t["dbg_e"][:, :],
                           in_=t["ebufAB"][:, 0, :]).then_inc(s_out.h, 16)
            s_out.n += 16

        tensor.wait_ge(s_ve.h, cp_mix)
        tensor.wait_ge(s_w.h, cp_wrt)
        for dj in range(8):
            g = gp_next()
            cp = gp_mm(g, [(t["wslot"][:, 0, cc, dj * 128:(dj + 1) * 128], t["xr_t"][:, cc, :])
                           for cc in range(8)])
            scalar.wait_ge(s_mm.h, cp)
            i = scalar.activation(t["r_t"][:, dj, :], GP[g], AF.Sigmoid)
            ac = s_ac.inc(i)
            vector.wait_ge(s_ac.h, ac)
            i = vector.tensor_scalar(t["rcp"][:, 0:1], t["rcp"][:, 0:1], 1.0, None, ALU.mult)
            gp_state["free"][g] = s_ve.inc(i)
        cp_r_mm = s_mm.n

        gpsimd.wait_ge(s_mm.h, cp_r_mm)
        cp_wo = wload(t["wslot"][:, 0, :, :], _w3(t["wo"]))

        tensor.wait_ge(s_w.h, cp_wkt)
        for dj in range(8):
            g = gp_next()
            cp = gp_mm(g, [(t["wslot"][:, 1, cc, dj * 128:(dj + 1) * 128], t["xk_t"][:, cc, :])
                           for cc in range(8)])
            vector.wait_ge(s_mm.h, cp)
            vector.tensor_scalar_max(t["k1"][:, :], GP[g], 0.0)
            i = vector.tensor_mul(t["kact_t"][:, dj, :], t["k1"][:, :], t["k1"][:, :])
            gp_state["free"][g] = s_ve.inc(i)
        cp_k_mm = s_mm.n
        wslot_cons[1] = cp_k_mm
        mix_gate[0] = cp_k_mm

        tensor.wait_ge(s_w.h, cp_wvt_res)
        for dj in range(8):
            g = gp_next()
            cp = gp_mm(g, [(t["wv_t"][:, cc, dj * 128:(dj + 1) * 128], t["kact_t"][:, cc, :])
                           for cc in range(8)])
            vector.wait_ge(s_mm.h, cp)
            i = vector.tensor_tensor(t["ffncol"][:, dj, :], GP[g], t["r_t"][:, dj, :], ALU.mult)
            gp_state["free"][g] = s_ve.inc(i)

        if os.environ.get("MKDBG") and qt == 0:
            sync.wait_ge(s_ve.h, s_ve.n)
            sync.dma_start(out=t["dbg_ffn"][:, :],
                           in_=t["ffncol"][:, :, :].rearrange("p a b -> p (a b)")).then_inc(s_out.h, 16)
            s_out.n += 16

        sync.wait_ge(s_mm.h, norm_state["bc"])
        for j in range(4):
            sync.wait_ge(s_ve.h, xslot_free[j])
            sync.dma_start(out=t["xtm"][:, j, :],
                           in_=t["xb"][q0 + j * 128: q0 + (j + 1) * 128, :]).then_inc(s_x.h, 16)
            s_x.n += 16
        cp_xrr = s_x.n

        cp_out_prev = s_out.n
        vector.wait_ge(s_out.h, cp_out_prev)
        tensor.wait_ge(s_ve.h, cp_attn)
        tensor.wait_ge(s_kw.h, norm_state["sh"])
        tensor.wait_ge(s_w.h, cp_wo)
        add_cp = {}
        comb_cp = {-1: 0}
        cpT = 0
        for fo in range(9):
            if fo < 8:
                g = 1 + (fo % 6)
                cpm = gp_mm(g, [(t["wslot"][:, 0, cc, fo * 128:(fo + 1) * 128],
                                 t["attncol"][:, cc, :]) for cc in range(8)])
            if fo >= 1:
                p = fo - 1
                tensor.wait_ge(s_ve.h, add_cp[p])
                tensor.wait_ge(s_ve.h, comb_cp[p - 1])
                for j in range(4):
                    i = tensor.transpose(t["G1"][:, j * 128:(j + 1) * 128],
                                         t["k1"][:, j * 128:(j + 1) * 128], t["idf"][:, :])
                cpT = s_mm.inc(i)
            if fo < 8:
                vector.wait_ge(s_mm.h, max(cpm, cpT))
                i = vector.tensor_tensor(t["k1"][:, :], GP[g], t["ffncol"][:, fo, :], ALU.add)
                add_cp[fo] = s_ve.inc(i)
                gp_state["free"][g] = add_cp[fo]
            if fo >= 1:
                p = fo - 1
                vector.wait_ge(s_mm.h, cpT)
                if p == 0:
                    vector.wait_ge(s_x.h, cp_xrr)
                i = vector.tensor_tensor(
                    t["outst"][:, :, p * 128:(p + 1) * 128],
                    t["G1"][:, :].rearrange("p (j f) -> p j f", j=4),
                    t["xtm"][:, :, p * 128:(p + 1) * 128], ALU.add)
                comb_cp[p] = s_ve.inc(i)
        wslot_cons[0] = s_mm.n
        cp_comb = s_ve.n
        norm_state["ve"] = cp_comb
        gp_state["free"][0] = cp_comb
        sync.wait_ge(s_ve.h, cp_comb)
        sync.dma_start(
            out=t["out"][qt * 512:(qt + 1) * 512, :].rearrange("(j p) f -> p j f", p=128),
            in_=t["outst"][:, :, :]).then_inc(s_out.h, 16)
        s_out.n += 16
        for j in range(4):
            xslot_free[j] = cp_comb

    sync.wait_ge(s_out.h, s_out.n)


_NC_CACHE = [None]


def kernel(L0_attn_x, L0_attn_w_qkv, L0_attn_w_o, L0_ffn_W_r, L0_ffn_W_k,
           L0_ffn_W_v, L0_ffn_mu_r, L0_ffn_mu_k):
    x = np.ascontiguousarray(np.asarray(L0_attn_x, dtype=np.float32))
    w_qkv = np.asarray(L0_attn_w_qkv, dtype=np.float32)
    NH, HD = 16, 64
    qkv3 = w_qkv.reshape(D, NH, 3 * HD)
    wqh = np.ascontiguousarray(qkv3[:, :, 0:HD].reshape(D, D))
    wkh = np.ascontiguousarray(qkv3[:, :, HD:2 * HD].reshape(D, D))
    wvh = np.ascontiguousarray(qkv3[:, :, 2 * HD:3 * HD].reshape(D, D))
    wrt = np.ascontiguousarray(np.asarray(L0_ffn_W_r, np.float32).T)
    wkt = np.ascontiguousarray(np.asarray(L0_ffn_W_k, np.float32).T)
    wvt = np.ascontiguousarray(np.asarray(L0_ffn_W_v, np.float32).T)
    wo_ = np.ascontiguousarray(np.asarray(L0_attn_w_o, np.float32))
    mu_ = np.ascontiguousarray(
        np.stack([np.asarray(L0_ffn_mu_r, np.float32), np.asarray(L0_ffn_mu_k, np.float32)]))

    if _NC_CACHE[0] is None:
        _NC_CACHE[0] = build()
    nc = _NC_CACHE[0]

    in_maps = []
    for i in range(8):
        b, p = i // 4, i % 4
        a0, b0 = p * CH, (7 - p) * CH
        xbound = np.zeros((2, D), np.float32)
        if a0 > 0:
            xbound[0] = x[b, a0 - 1]
        xbound[1] = x[b, b0 - 1]
        in_maps.append(dict(
            xb=x[b], xbound=xbound, wqh=wqh, wkh=wkh, wvh=wvh,
            wrt=wrt, wkt=wkt, wvt=wvt, wo=wo_, mu=mu_))
    res = run_bass_kernel_spmd(nc, in_maps, core_ids=list(range(8)))
    outp = np.empty((2, T, D), np.float32)
    for i in range(8):
        b, p = i // 4, i % 4
        o = res.results[i]["out"]
        outp[b, p * CH:(p + 1) * CH] = o[0:CH]
        outp[b, (7 - p) * CH:(8 - p) * CH] = o[CH:2 * CH]
    return outp


# version 29 (final)
# speedup vs baseline: 1.6230x; 1.0160x over previous
import os
import numpy as np
import concourse.bass as bass
import concourse.mybir as mybir
from concourse.bass_utils import run_bass_kernel_spmd

F32 = mybir.dt.float32
BF16 = mybir.dt.bfloat16
AF = mybir.ActivationFunctionType
ALU = mybir.AluOpType
AX = mybir.AxisListType

D = 1024
T = 8192
CH = 1024
QT = 512
KC = 128
SUP = 8
NHP = 8
EPS = 1e-5
SCALE = 1.0 / 32.0


class Sem:
    def __init__(self, ctx, nc, name):
        self.h = ctx.enter_context(nc.semaphore(name))
        self.n = 0

    def inc(self, instr, k=1):
        instr.then_inc(self.h, k)
        self.n += k
        return self.n


def _w3(src):
    return src.rearrange("(c p) f -> p c f", p=128)


def build():
    nc = bass.Bass("TRN2", target_bir_lowering=False)

    d_in = {}
    for name in ["wqh", "wkh", "wvh", "wrt", "wkt", "wvt", "wo"]:
        d_in[name] = nc.dram_tensor(name, [D, D], F32, kind="ExternalInput").ap()
    d_in["xb"] = nc.dram_tensor("xb", [T, D], F32, kind="ExternalInput").ap()
    d_in["xbound"] = nc.dram_tensor("xbound", [2, D], F32, kind="ExternalInput").ap()
    d_in["mu"] = nc.dram_tensor("mu", [2, D], F32, kind="ExternalInput").ap()
    d_in["out"] = nc.dram_tensor("out", [2 * CH, D], F32, kind="ExternalOutput").ap()
    d_in["kvk"] = nc.dram_tensor("kvk", [D, T], BF16, kind="Internal").ap()
    d_in["kvv"] = nc.dram_tensor("kvv", [T, D], BF16, kind="Internal").ap()
    if os.environ.get("MKDBG"):
        d_in["dbg_xn"] = nc.dram_tensor("dbg_xn", [128, 2 * 8 * 1026], BF16, kind="ExternalOutput").ap()
        d_in["dbg_muT"] = nc.dram_tensor("dbg_muT", [128, 32], F32, kind="ExternalOutput").ap()
        d_in["dbg_q"] = nc.dram_tensor("dbg_q", [128, 8 * 512], BF16, kind="ExternalOutput").ap()
        d_in["dbg_kvk"] = nc.dram_tensor("dbg_kvk", [D, 1024], BF16, kind="ExternalOutput").ap()
        d_in["dbg_kvv"] = nc.dram_tensor("dbg_kvv", [1024, D], BF16, kind="ExternalOutput").ap()
        d_in["dbg_att"] = nc.dram_tensor("dbg_att", [128, 8 * 512], BF16, kind="ExternalOutput").ap()
        d_in["dbg_ffn"] = nc.dram_tensor("dbg_ffn", [128, 8 * 512], BF16, kind="ExternalOutput").ap()
        d_in["dbg_e"] = nc.dram_tensor("dbg_e", [128, 2 * 512], BF16, kind="ExternalOutput").ap()
        d_in["dbg_oa"] = nc.dram_tensor("dbg_oa", [128, 512], F32, kind="ExternalOutput").ap()
        d_in["dbg_ob"] = nc.dram_tensor("dbg_ob", [128, 512], F32, kind="ExternalOutput").ap()
        d_in["dbg_atm"] = nc.dram_tensor("dbg_atm", [128, 512], BF16, kind="ExternalOutput").ap()

    from contextlib import ExitStack
    with ExitStack() as ctx:
        E_ = ctx.enter_context
        t = dict(d_in)

        t["xn_own"] = E_(nc.sbuf_tensor("xn_own", [128, 2, 8, 1026], BF16))
        t["wq"] = E_(nc.sbuf_tensor("wq", [128, 8, D], BF16))
        t["wslot"] = E_(nc.sbuf_tensor("wslot", [128, 2, 8, D], BF16))
        t["qcol"] = E_(nc.sbuf_tensor("qcol", [128, 8, QT], BF16))
        t["xtm"] = E_(nc.sbuf_tensor("xtm", [128, 4, D], F32))
        t["ksup"] = E_(nc.sbuf_tensor("ksup", [128, 2, SUP * KC], BF16))
        t["vsup"] = E_(nc.sbuf_tensor("vsup", [128, 2, SUP, 130], BF16))
        t["ebufAB"] = E_(nc.sbuf_tensor("ebufAB", [128, 4, 2 * QT], BF16))
        t["oa"] = E_(nc.sbuf_tensor("oa", [128, QT], F32))
        t["atm"] = E_(nc.sbuf_tensor("atm", [128, 512], BF16))
        t["onesn"] = E_(nc.sbuf_tensor("onesn", [128, 64], F32))
        t["outst"] = E_(nc.sbuf_tensor("outst", [128, 4, D], F32))
        t["msk"] = E_(nc.sbuf_tensor("msk", [128, 896], BF16))
        t["idf"] = E_(nc.sbuf_tensor("idf", [128, 128], F32))
        t["idb"] = E_(nc.sbuf_tensor("idb", [128, 128], BF16))
        t["muT"] = E_(nc.sbuf_tensor("muT", [128, 4, 8], F32))
        t["st6"] = E_(nc.sbuf_tensor("st6", [128, 12], F32))
        t["stt"] = E_(nc.sbuf_tensor("stt", [128, 24], F32))
        t["mv2"] = E_(nc.sbuf_tensor("mv2", [128, 2], F32))
        t["rstd"] = E_(nc.sbuf_tensor("rstd", [128, 1], F32))
        t["rcp"] = E_(nc.sbuf_tensor("rcp", [128, 1], F32))
        t["onesf"] = E_(nc.sbuf_tensor("onesf", [1, 2], F32))
        t["onesb"] = E_(nc.sbuf_tensor("onesb", [1, 2], BF16))
        ar1 = E_(nc.sbuf_tensor("ar1", [128, 8192], BF16))
        ar2 = E_(nc.sbuf_tensor("ar2", [128, 16384], BF16))
        ar3 = E_(nc.sbuf_tensor("ar3", [128, 8192], BF16))

        t["xnring"] = ar1[:, :].rearrange("p (s c t) -> p s c t", s=2, c=8, t=512)
        t["xr_t"] = ar1[:, 0:4096].rearrange("p (c t) -> p c t", c=8, t=512)
        t["xk_t"] = ar1[:, 4096:8192].rearrange("p (c t) -> p c t", c=8, t=512)
        t["wk_t"] = ar2[:, 0:8192].rearrange("p (c f) -> p c f", c=8, f=1024)
        t["wv_t"] = ar2[:, 8192:16384].rearrange("p (c f) -> p c f", c=8, f=1024)
        t["r_t"] = ar2[:, 0:4096].rearrange("p (c t) -> p c t", c=8, t=512)
        t["kact_t"] = ar2[:, 4096:8192].rearrange("p (c t) -> p c t", c=8, t=512)
        t["kst"] = ar3[:, 0:4096].rearrange("p (c t) -> p c t", c=8, t=512)
        t["vst"] = ar3[:, 4096:8192].rearrange("p (c t) -> p c t", c=8, t=512)
        t["attncol"] = ar3[:, 0:4096].rearrange("p (c t) -> p c t", c=8, t=512)
        t["ffncol"] = ar3[:, 4096:8192].rearrange("p (c t) -> p c t", c=8, t=512)
        t["k1"] = t["oa"][:, :]
        t["rb"] = t["xtm"][:, 0, :]
        t["aev"] = t["outst"][:, 0, 0:QT]
        t["xnb0"] = ar3[0:1, 0:D]
        t["xnb1"] = ar3[0:1, D:2 * D]
        t["xbf"] = None

        t["SAB"] = E_(nc.psum_tensor("SAB", [128, 4 * QT], F32))
        t["AVa"] = E_(nc.psum_tensor("AVa", [128, QT], F32))
        t["AVb"] = E_(nc.psum_tensor("AVb", [128, QT], F32))
        t["G1"] = E_(nc.psum_tensor("G1", [128, QT], F32))
        t["G2"] = E_(nc.psum_tensor("G2", [128, 1024], BF16))

        sems = {k: Sem(ctx, nc, k) for k in
                ["s_x", "s_w", "s_kw", "s_kr", "s_mm", "s_ve", "s_ac", "s_out"]}

        engines = [nc.sync, nc.tensor, nc.vector, nc.scalar, nc.gpsimd]
        pid = nc.partition_id([e.engine for e in engines])
        idx = [pid % 4 for _ in engines]
        for p in nc.Switch(engines=engines, index=idx, n=4):
            emit_body(nc, p, sems, t)
    return nc


def emit_body(nc, p, sems, t):
    for s in sems.values():
        s.n = 0
    s_x, s_w, s_kw, s_kr = sems["s_x"], sems["s_w"], sems["s_kw"], sems["s_kr"]
    s_mm, s_ve, s_ac, s_out = sems["s_mm"], sems["s_ve"], sems["s_ac"], sems["s_out"]
    sync, tensor, vector, scalar, gpsimd = nc.sync, nc.tensor, nc.vector, nc.scalar, nc.gpsimd

    chunkA, chunkB = p, 7 - p
    PC = 8 - p
    NSUB = 2 * PC
    own_subs = {2 * chunkA: (0, 0), 2 * chunkA + 1: (0, 512),
                2 * chunkB: (1, 0), 2 * chunkB + 1: (1, 512)}

    GP = [t["G1"][:, :], t["AVa"][:, :], t["AVb"][:, :],
          t["SAB"][:, 0:QT], t["SAB"][:, QT:2 * QT],
          t["SAB"][:, 2 * QT:3 * QT], t["SAB"][:, 3 * QT:4 * QT]]
    gp_state = {"i": 0, "free": [0] * len(GP)}
    wslot_cons = [0, 0]

    def gp_next():
        g = gp_state["i"] % len(GP)
        gp_state["i"] += 1
        return g

    def gp_mm(g, lhsT_rhs_list, out=None):
        tensor.wait_ge(s_ve.h, gp_state["free"][g])
        dst = GP[g] if out is None else out
        n = len(lhsT_rhs_list)
        for j, (lh, rh) in enumerate(lhsT_rhs_list):
            i = tensor.matmul(dst, lh, rh, start=(j == 0), stop=(j == n - 1))
        return s_mm.inc(i)

    def gp_evac(g, cp, dest, src=None):
        vector.wait_ge(s_mm.h, cp)
        i = vector.tensor_copy(dest, GP[g] if src is None else src)
        gp_state["free"][g] = s_ve.inc(i)
        return gp_state["free"][g]

    iosc = t["outst"][:, 0, 0:896]
    murow = t["outst"][0:1, 1:3, :].rearrange("p a b -> p (a b)")

    i = gpsimd.iota(iosc, [[1, 896]], channel_multiplier=-1,
                    allow_small_or_imprecise_dtypes=True)
    gpsimd.memset(t["onesf"][:, :], 1.0)
    gpsimd.memset(t["onesn"][:, :], 1.0)
    i2 = gpsimd.memset(t["onesb"][:, :], 1.0)
    s_w.inc(i2)
    cp_iota = s_w.n

    def wload(dst_ap, src_ap):
        gpsimd.dma_start(out=dst_ap, in_=src_ap).then_inc(s_w.h, 16)
        s_w.n += 16
        return s_w.n

    wload(t["wq"][:, :, :], _w3(t["wqh"]))
    wload(t["wk_t"], _w3(t["wkh"]))
    cp_w_kv = wload(t["wv_t"], _w3(t["wvh"]))
    gpsimd.dma_start(out=murow[0:1, 0:D], in_=t["mu"][0:1, :]).then_inc(s_w.h, 16)
    s_w.n += 16
    gpsimd.dma_start(out=murow[0:1, D:2 * D], in_=t["mu"][1:2, :]).then_inc(s_w.h, 16)
    s_w.n += 16
    cp_mu_dma = s_w.n

    vector.wait_ge(s_w.h, cp_iota)
    vector.tensor_scalar(t["msk"][:, :], iosc, 384.0, None, ALU.is_ge)
    vector.tensor_scalar(t["idf"][:, :], iosc[:, 0:128], 0.0, None, ALU.is_equal)
    vector.tensor_scalar(t["idb"][:, :], iosc[:, 0:128], 0.0, None, ALU.is_equal)
    for slot in range(2):
        vector.memset(t["vsup"][:, slot, :, 64:65], 1.0)
        vector.memset(t["vsup"][:, slot, :, 129:130], 1.0)
    tensor.wait_ge(s_w.h, cp_mu_dma)
    for v in range(2):
        off = v * D
        for cc in range(8):
            g = gp_next()
            cp = gp_mm(g, [(murow[0:1, off + cc * 128: off + (cc + 1) * 128],
                            t["onesf"][0:1, 0:1])], out=GP[g][:, 0:1])
            gp_evac(g, cp, t["muT"][:, v, cc:cc + 1], src=GP[g][:, 0:1])
    vector.tensor_scalar(t["muT"][:, 2, :], t["muT"][:, 0, :], -1.0, 1.0, ALU.mult, ALU.add)
    i = vector.tensor_scalar(t["muT"][:, 3, :], t["muT"][:, 1, :], -1.0, 1.0, ALU.mult, ALU.add)
    cp_mu_ve = s_ve.inc(i)

    sqjunk = t["outst"][:, 0, :]

    def ln_row(src1d, dest_bf):
        i = vector.reduce_sum(t["st6"][0:1, 0:1], src1d[0:1, :], axis=AX.X)
        s_ve.inc(i)
        scalar.wait_ge(s_ve.h, s_ve.n)
        i = scalar.activation(sqjunk[0:1, :], src1d[0:1, :], AF.Square,
                              accum_out=t["st6"][0:1, 1:2])
        s_ac.inc(i)
        vector.wait_ge(s_ac.h, s_ac.n)
        vector.tensor_scalar(t["mv2"][0:1, 0:1], t["st6"][0:1, 0:1], 1.0 / 1024.0, None, ALU.mult)
        vector.tensor_mul(t["st6"][0:1, 2:3], t["mv2"][0:1, 0:1], t["mv2"][0:1, 0:1])
        vector.tensor_scalar(t["mv2"][0:1, 1:2], t["st6"][0:1, 1:2], 1.0 / 1024.0, None, ALU.mult)
        vector.tensor_tensor(t["mv2"][0:1, 1:2], t["mv2"][0:1, 1:2], t["st6"][0:1, 2:3], ALU.subtract)
        i = vector.tensor_scalar_add(t["mv2"][0:1, 1:2], t["mv2"][0:1, 1:2], EPS)
        s_ve.inc(i)
        scalar.wait_ge(s_ve.h, s_ve.n)
        scalar.activation(t["rstd"][0:1, :], t["mv2"][0:1, 1:2], AF.Ln)
        i = scalar.activation(t["rstd"][0:1, :], t["rstd"][0:1, :], AF.Exp, scale=-0.5)
        s_ac.inc(i)
        vector.wait_ge(s_ac.h, s_ac.n)
        i = vector.tensor_scalar(dest_bf[0:1, :], src1d[0:1, :], t["mv2"][0:1, 0:1],
                                 t["rstd"][0:1, 0:1], ALU.subtract, ALU.mult)
        return s_ve.inc(i)

    sync.wait_ge(s_ve.h, cp_mu_ve)
    sync.dma_start(out=murow[0:1, 0:D], in_=t["xbound"][0:1, :]).then_inc(s_x.h, 16)
    s_x.n += 16
    sync.dma_start(out=murow[0:1, D:2 * D], in_=t["xbound"][1:2, :]).then_inc(s_x.h, 16)
    s_x.n += 16
    vector.wait_ge(s_x.h, s_x.n)
    ln_row(murow[0:1, 0:D], t["xnb0"])
    cp_b = ln_row(murow[0:1, D:2 * D], t["xnb1"])
    tensor.wait_ge(s_ve.h, cp_b)
    for ci, xnb in ((0, t["xnb0"]), (1, t["xnb1"])):
        for cc in range(8):
            g = gp_next()
            cp = gp_mm(g, [(xnb[0:1, cc * 128:(cc + 1) * 128], t["onesb"][0:1, 0:1])],
                       out=GP[g][:, 0:1])
            gp_evac(g, cp, t["xn_own"][:, ci, cc, 0:1], src=GP[g][:, 0:1])

    tensor.wait_ge(s_w.h, cp_w_kv)
    stt = t["stt"]
    g2free = [0, 0]
    stage_free = {0: 0, 1: 0}
    store_gate = {"k": 0, "v": 0}
    slotB = {}
    apply_cp = {}
    stats_cp = {}
    xn_done = {}
    dma_cp = {}

    def xn_view(s):
        if s in own_subs:
            ci, toff = own_subs[s]
            return t["xn_own"][:, ci, :, 1 + toff: 1 + toff + 512]
        return t["xnring"][:, s % 2, :, 0:512]

    def emit_dma(s):
        for j in range(4):
            if s >= 1:
                sync.wait_ge(s_mm.h, slotB[(s - 1, j)])
            sync.dma_start(out=t["xtm"][:, j, :],
                           in_=t["xb"][s * 512 + j * 128: s * 512 + (j + 1) * 128, :]
                           ).then_inc(s_x.h, 16)
            s_x.n += 16
        dma_cp[s] = s_x.n

    def emit_stats(s):
        vector.wait_ge(s_x.h, dma_cp[s])
        for j in range(4):
            i = vector.reduce_sum(stt[:, j:j + 1], t["xtm"][:, j, :], axis=AX.X)
        cp_rd = s_ve.inc(i)
        scalar.wait_ge(s_x.h, dma_cp[s])
        for j in range(4):
            i = scalar.activation(sqjunk, t["xtm"][:, j, :], AF.Square,
                                  accum_out=stt[:, 4 + j:5 + j])
        cp_sq = s_ac.inc(i)
        vector.wait_ge(s_ac.h, cp_sq)
        vector.wait_ge(s_ve.h, cp_rd)
        i = vector.tensor_scalar(stt[:, 8:12], stt[:, 0:4], 1.0 / 1024.0, None, ALU.mult)
        vector.wait_ge(s_ve.h, s_ve.inc(i))
        i = vector.tensor_mul(stt[:, 12:16], stt[:, 8:12], stt[:, 8:12])
        s_ve.inc(i)
        i = vector.tensor_scalar(stt[:, 20:24], stt[:, 4:8], 1.0 / 1024.0, None, ALU.mult)
        vector.wait_ge(s_ve.h, s_ve.inc(i))
        i = vector.tensor_tensor(stt[:, 12:16], stt[:, 20:24], stt[:, 12:16], ALU.subtract)
        vector.wait_ge(s_ve.h, s_ve.inc(i))
        i = vector.tensor_scalar_add(stt[:, 12:16], stt[:, 12:16], EPS)
        cp_v = s_ve.inc(i)
        scalar.wait_ge(s_ve.h, cp_v)
        scalar.activation(stt[:, 16:20], stt[:, 12:16], AF.Ln)
        i = scalar.activation(stt[:, 16:20], stt[:, 16:20], AF.Exp, scale=-0.5)
        stats_cp[s] = s_ac.inc(i)

    def emit_apply(s, tt):
        slot = tt % 2
        stage = t["ebufAB"][:, slot, :]
        vector.wait_ge(s_ac.h, stats_cp[s])
        vector.wait_ge(s_mm.h, stage_free[slot])
        i = vector.tensor_scalar(stage, t["xtm"][:, tt, :], stt[:, 8 + tt:9 + tt],
                                 stt[:, 16 + tt:17 + tt], ALU.subtract, ALU.mult)
        apply_cp[(s, tt)] = s_ve.inc(i)

    def emit_T(s, tt):
        slot = tt % 2
        stage = t["ebufAB"][:, slot, :]
        dv = xn_view(s)[:, :, tt * 128:(tt + 1) * 128]
        tensor.wait_ge(s_ve.h, max(apply_cp[(s, tt)], g2free[0]))
        for j in range(8):
            i = tensor.transpose(t["G2"][:, j * 128:(j + 1) * 128],
                                 stage[:, j * 128:(j + 1) * 128], t["idb"][:, :])
        cpB = s_mm.inc(i)
        slotB[(s, tt)] = cpB
        stage_free[slot] = cpB
        vector.wait_ge(s_mm.h, cpB)
        i = vector.tensor_copy(dv[:, :, :],
                               t["G2"][:, :].rearrange("p (c f) -> p c f", c=8, f=128))
        g2free[0] = s_ve.inc(i)
        if tt == 3:
            xn_done[s] = g2free[0]
        if tt == 0:
            emit_apply(s, 2)
        elif tt == 1:
            emit_apply(s, 3)

    def emit_kv_group(sI, g):
        xnsrc = xn_view(sI)
        if g == 0:
            tensor.wait_ge(s_ve.h, xn_done[sI])
        if g < 8:
            fj = g
            gidx = gp_next()
            cp = gp_mm(gidx, [(t["wk_t"][:, cc, fj * 128:(fj + 1) * 128], xnsrc[:, cc, :])
                              for cc in range(8)])
            if g == 0:
                vector.wait_ge(s_kw.h, store_gate["k"])
            cpk = gp_evac(gidx, cp, t["kst"][:, fj, :])
            if g == 7:
                gpsimd.wait_ge(s_ve.h, cpk)
                gpsimd.dma_start(
                    out=t["kvk"].rearrange("(c p) t -> p c t", p=128)[:, :, sI * 512:(sI + 1) * 512],
                    in_=t["kst"][:, :, :]).then_inc(s_kw.h, 16)
                s_kw.n += 16
                store_gate["k"] = s_kw.n
        else:
            tj, dh = (g - 8) // 2, (g - 8) % 2
            gidx = gp_next()
            cp = gp_mm(gidx, [(xnsrc[:, cc, tj * 128:(tj + 1) * 128],
                               t["wv_t"][:, cc, dh * 512:(dh + 1) * 512])
                              for cc in range(8)])
            if g == 8:
                vector.wait_ge(s_kw.h, store_gate["v"])
            cpv = gp_evac(gidx, cp, t["vst"][:, tj * 2 + dh, :])
            if g == 15:
                gpsimd.wait_ge(s_ve.h, cpv)
                gpsimd.dma_start(
                    out=t["kvv"][sI * 512:(sI + 1) * 512, :].rearrange(
                        "(j p) (h f) -> p j h f", p=128, h=2),
                    in_=t["vst"][:, :, :].rearrange("p (j h) f -> p j h f", h=2)
                ).then_inc(s_kw.h, 16)
                s_kw.n += 16
                store_gate["v"] = s_kw.n

    emit_dma(0)
    emit_stats(0)
    emit_apply(0, 0)
    emit_apply(0, 1)
    for it in range(NSUB + 1):
        sT, sKV = it, it - 1
        if sKV < 0:
            for tt in range(4):
                emit_T(sT, tt)
        elif sT < NSUB:
            for g in range(16):
                emit_kv_group(sKV, g)
                if g in (3, 6, 9, 12):
                    emit_T(sT, (3, 6, 9, 12).index(g))
        else:
            for g in range(16):
                emit_kv_group(sKV, g)
        if sT + 1 <= NSUB - 1:
            emit_dma(sT + 1)
            emit_stats(sT + 1)
            emit_apply(sT + 1, 0)
            emit_apply(sT + 1, 1)

    if os.environ.get("MKDBG"):
        sync.wait_ge(s_ve.h, xn_done[NSUB - 1])
        sync.dma_start(out=t["dbg_xn"][:, :],
                       in_=t["xn_own"][:, :, :, :].rearrange("p a b c -> p (a b c)")
                       ).then_inc(s_out.h, 16)
        s_out.n += 16
        sync.dma_start(out=t["dbg_muT"][:, :],
                       in_=t["muT"][:, :, :].rearrange("p a b -> p (a b)")).then_inc(s_out.h, 16)
        s_out.n += 16

    xslot_free = [0] * 4
    cp_ph2_mm = s_mm.n
    gpsimd.wait_ge(s_mm.h, cp_ph2_mm)
    cp_wvt_res = wload(t["wv_t"], _w3(t["wvt"]))
    mix_gate = [cp_ph2_mm]
    cp_kv_done = s_kw.n
    sync.wait_ge(s_kw.h, cp_kv_done)
    if os.environ.get("MKDBG"):
        sync.dma_start(out=t["dbg_kvk"][:, :], in_=t["kvk"][:, 0:1024]).then_inc(s_out.h, 16)
        s_out.n += 16
        sync.dma_start(out=t["dbg_kvv"][:, :], in_=t["kvv"][0:1024, :]).then_inc(s_out.h, 16)
        s_out.n += 16

    norm_state = {"ve": 0, "sh": s_kw.n, "bc": 0, "pend_rc": None, "pend": None,
                  "scrd": 0}
    sup = {"last": [0, 0], "cp": {}, "base": {}}
    _run = 0
    for q_ in range(4):
        _ch = (chunkA if q_ < 2 else chunkB)
        _ns = ((_ch * CH + (q_ % 2) * 512) // KC + 4 + SUP - 1) // SUP
        for h_ in range(NHP):
            sup["base"][(q_, h_)] = _run % 2
            _run += _ns

    def issue_sup_g(q_, h_, ks):
        slot = (sup["base"][(q_, h_)] + ks) % 2
        sync.wait_ge(s_mm.h, sup["last"][slot])
        sync.dma_start(out=t["ksup"][:, slot, :],
                       in_=t["kvk"][h_ * 128:(h_ + 1) * 128,
                                    ks * 1024:(ks + 1) * 1024]).then_inc(s_kr.h, 16)
        s_kr.n += 16
        for b_ in range(2):
            sync.dma_start(
                out=t["vsup"][:, slot, :, b_ * 65: b_ * 65 + 64],
                in_=t["kvv"][ks * 1024:(ks + 1) * 1024,
                             h_ * 128 + b_ * 64: h_ * 128 + b_ * 64 + 64]
                    .rearrange("(a p) f -> p a f", p=128)).then_inc(s_kr.h, 16)
            s_kr.n += 16
        sup["cp"][(q_, h_, ks)] = s_kr.n

    for qt in range(4):
        qt_out_gate = s_out.n
        ci = 0 if qt < 2 else 1
        chunk = chunkA if qt < 2 else chunkB
        toff = (qt % 2) * 512
        q0 = chunk * CH + toff
        kc_end = q0 // KC + 4
        nsup = (kc_end + SUP - 1) // SUP
        xcur = t["xn_own"][:, ci, :, 1 + toff: 1 + toff + 512]
        xprev = t["xn_own"][:, ci, :, toff: toff + 512]

        for fj in range(8):
            g = gp_next()
            cp = gp_mm(g, [(t["wq"][:, cc, fj * 128:(fj + 1) * 128], xcur[:, cc, :])
                           for cc in range(8)])
            cp_q = gp_evac(g, cp, t["qcol"][:, fj, :])

        if os.environ.get("MKDBG") and qt == 0:
            sync.wait_ge(s_ve.h, cp_q)
            sync.dma_start(out=t["dbg_q"][:, :],
                           in_=t["qcol"][:, :, :].rearrange("p a b -> p (a b)")).then_inc(s_out.h, 16)
            s_out.n += 16

        vector.wait_ge(s_mm.h, mix_gate[0])
        for cc in range(8):
            vector.tensor_scalar(t["xr_t"][:, cc, :], xcur[:, cc, :],
                                 t["muT"][:, 0, cc:cc + 1], None, ALU.mult)
            vector.scalar_tensor_tensor(t["xr_t"][:, cc, :], xprev[:, cc, :],
                                        t["muT"][:, 2, cc:cc + 1], t["xr_t"][:, cc, :],
                                        ALU.mult, ALU.add)
            vector.tensor_scalar(t["xk_t"][:, cc, :], xcur[:, cc, :],
                                 t["muT"][:, 1, cc:cc + 1], None, ALU.mult)
            i = vector.scalar_tensor_tensor(t["xk_t"][:, cc, :], xprev[:, cc, :],
                                            t["muT"][:, 3, cc:cc + 1], t["xk_t"][:, cc, :],
                                            ALU.mult, ALU.add)
        cp_mix = s_ve.inc(i)
        gpsimd.wait_ge(s_mm.h, wslot_cons[0])
        cp_wrt = wload(t["wslot"][:, 0, :, :], _w3(t["wrt"]))
        gpsimd.wait_ge(s_mm.h, wslot_cons[1])
        cp_wkt = wload(t["wslot"][:, 1, :, :], _w3(t["wkt"]))

        for hp in range(NHP):
            sc_cp = {}
            av_cp = {}
            e_cp = {}
            sbase = sup["base"][(qt, hp)]

            if (qt, hp, 0) not in sup["cp"]:
                issue_sup_g(qt, hp, 0)
            if nsup > 1 and (qt, hp, 1) not in sup["cp"]:
                issue_sup_g(qt, hp, 1)
            tensor.wait_ge(s_ve.h, cp_q)

            def em_scores(kc):
                half = kc % 2
                ks = kc // SUP
                tensor.wait_ge(s_kr.h, sup["cp"][(qt, hp, ks)])
                if kc == 2 and norm_state["pend"]:
                    tensor.wait_ge(s_ve.h, norm_state["pend"][1])
                if kc >= 2:
                    tensor.wait_ge(s_ac.h, e_cp[kc - 2][2])
                slot = (sbase + ks) % 2
                tensor.matmul(t["SAB"][:, half * 2 * QT: half * 2 * QT + QT],
                              t["ksup"][0:64, slot, (kc % SUP) * KC:(kc % SUP + 1) * KC],
                              t["qcol"][0:64, hp, :], start=True, stop=True)
                i = tensor.matmul(t["SAB"][:, half * 2 * QT + QT: (half + 1) * 2 * QT],
                                  t["ksup"][64:128, slot, (kc % SUP) * KC:(kc % SUP + 1) * KC],
                                  t["qcol"][64:128, hp, :], start=True, stop=True)
                sc_cp[kc] = s_mm.inc(i)

            def em_exp(kc):
                half = kc % 2
                eh = kc % 4
                scalar.wait_ge(s_mm.h, sc_cp[kc])
                i = scalar.activation(t["ebufAB"][:, eh, :],
                                      t["SAB"][:, half * 2 * QT:(half + 1) * 2 * QT],
                                      AF.Exp, scale=SCALE)
                ac = s_ac.inc(i)
                c = kc * KC - q0
                if 0 <= c < 512:
                    vector.wait_ge(s_ac.h, ac)
                    m = t["msk"][:, 384 - c: 896 - c]
                    vector.tensor_tensor(t["ebufAB"][:, eh, 0:QT],
                                         t["ebufAB"][:, eh, 0:QT], m, ALU.mult)
                    i = vector.tensor_tensor(t["ebufAB"][:, eh, QT:2 * QT],
                                             t["ebufAB"][:, eh, QT:2 * QT], m, ALU.mult)
                    e_cp[kc] = ("ve", s_ve.inc(i), ac)
                else:
                    e_cp[kc] = ("ac", ac, ac)

            def em_av(kc):
                eh = kc % 4
                kind, cpv, _ = e_cp[kc]
                tensor.wait_ge(s_ac.h if kind == "ac" else s_ve.h, cpv)
                ks = kc // SUP
                slot = (sbase + ks) % 2
                vk = t["vsup"][:, slot, kc % SUP, :]
                tensor.matmul(t["AVa"][0:65, :], vk[:, 0:65], t["ebufAB"][:, eh, 0:QT],
                              start=(kc == 0), stop=(kc == kc_end - 1))
                i = tensor.matmul(t["AVb"][0:65, :], vk[:, 65:130], t["ebufAB"][:, eh, QT:2 * QT],
                                  start=(kc == 0), stop=(kc == kc_end - 1))
                av_cp[kc] = s_mm.inc(i)
                sup["last"][slot] = av_cp[kc]
                if (kc + 1) % SUP == 0:
                    ks_next = kc // SUP + 2
                    if ks_next < nsup:
                        issue_sup_g(qt, hp, ks_next)

            em_scores(0)
            em_exp(0)
            em_scores(1)
            em_exp(1)
            def emit_norm_tail():
                if norm_state["pend_rc"] is None:
                    return
                php, rkind, cp_rc = norm_state["pend_rc"]
                norm_state["pend_rc"] = None
                tensor.wait_ge(s_ac.h if rkind == "ac" else s_ve.h, cp_rc)
                tensor.matmul(t["G1"][0:64, :], t["onesn"][64:65, :],
                              t["rb"][64:65, 0:512], start=True, stop=True)
                i = tensor.matmul(t["SAB"][0:64, 0:QT], t["onesn"][64:65, :],
                                  t["rb"][64:65, 512:1024], start=True, stop=True)
                cp_bc = s_mm.inc(i)
                vector.wait_ge(s_mm.h, cp_bc)
                i = vector.tensor_tensor(t["attncol"][0:64, php, :], t["oa"][0:64, :],
                                         t["G1"][0:64, :], ALU.mult)
                cp_mA = s_ve.inc(i)
                vector.wait_ge(s_kw.h, norm_state["sh"])
                i = vector.tensor_tensor(t["atm"][0:64, :], t["aev"][0:64, :],
                                         t["SAB"][0:64, 0:QT], ALU.mult)
                cp_mB = s_ve.inc(i)
                gpsimd.wait_ge(s_ve.h, cp_mB)
                gpsimd.dma_start(out=t["attncol"][64:128, php, :],
                                 in_=t["atm"][0:64, :]).then_inc(s_kw.h, 16)
                s_kw.n += 16
                norm_state["sh"] = s_kw.n
                norm_state["bc"] = cp_bc
                norm_state["pend"] = (True, cp_mB)
                gp_state["free"][0] = cp_mA
                gp_state["free"][3] = cp_mB

            av_gate_done = False
            for kc in range(kc_end):
                if not av_gate_done:
                    tensor.wait_ge(s_ve.h, norm_state["ve"])
                    if norm_state["scrd"]:
                        tensor.wait_ge(s_ac.h, norm_state["scrd"])
                    av_gate_done = True
                em_av(kc)
                if kc == 0:
                    emit_norm_tail()
                if kc + 2 < kc_end:
                    em_scores(kc + 2)
                    em_exp(kc + 2)

            vector.wait_ge(s_mm.h, av_cp[kc_end - 1])
            if hp == 0:
                vector.wait_ge(s_out.h, qt_out_gate)
            small = kc_end <= 16
            if not small:
                vector.reciprocal(t["rb"][64:65, 0:512], t["AVa"][64:65, :])
                i = vector.reciprocal(t["rb"][64:65, 512:1024], t["AVb"][64:65, :])
                norm_state["pend_rc"] = (hp, "ve", s_ve.inc(i))
            vector.tensor_copy(t["oa"][0:65, :], t["AVa"][0:65, :])
            i = vector.tensor_copy(t["aev"][0:65, :], t["AVb"][0:65, :])
            cp_ev = s_ve.inc(i)
            if small:
                scalar.wait_ge(s_ve.h, cp_ev)
                scalar.wait_ge(s_ve.h, xslot_free[0])
                sc2 = t["outst"][64:65, 1, 0:512]
                scalar.activation(sc2, t["oa"][64:65, :], AF.Ln)
                scalar.activation(t["rb"][64:65, 0:512], sc2, AF.Exp, scale=-1.0)
                scalar.activation(sc2, t["aev"][64:65, :], AF.Ln)
                i = scalar.activation(t["rb"][64:65, 512:1024], sc2, AF.Exp, scale=-1.0)
                norm_state["pend_rc"] = (hp, "ac", s_ac.inc(i))
            norm_state["ve"] = cp_ev
            gp_state["free"][1] = cp_ev
            gp_state["free"][2] = cp_ev
            if hp + 1 < NHP:
                issue_sup_g(qt, hp + 1, 0)
            elif qt + 1 < 4:
                issue_sup_g(qt + 1, 0, 0)
        emit_norm_tail()
        cp_attn = s_ve.n
        if os.environ.get("MKDBG") and qt == 0:
            sync.wait_ge(s_ve.h, cp_attn)
            sync.dma_start(out=t["dbg_att"][:, :],
                           in_=t["attncol"][:, :, :].rearrange("p a b -> p (a b)")).then_inc(s_out.h, 16)
            s_out.n += 16
            sync.dma_start(out=t["dbg_e"][:, :],
                           in_=t["ebufAB"][:, 0, :]).then_inc(s_out.h, 16)
            s_out.n += 16

        tensor.wait_ge(s_ve.h, cp_mix)
        tensor.wait_ge(s_w.h, cp_wrt)
        for dj in range(8):
            g = gp_next()
            cp = gp_mm(g, [(t["wslot"][:, 0, cc, dj * 128:(dj + 1) * 128], t["xr_t"][:, cc, :])
                           for cc in range(8)])
            scalar.wait_ge(s_mm.h, cp)
            i = scalar.activation(t["r_t"][:, dj, :], GP[g], AF.Sigmoid)
            ac = s_ac.inc(i)
            vector.wait_ge(s_ac.h, ac)
            i = vector.tensor_scalar(t["rcp"][:, 0:1], t["rcp"][:, 0:1], 1.0, None, ALU.mult)
            gp_state["free"][g] = s_ve.inc(i)
        cp_r_mm = s_mm.n

        gpsimd.wait_ge(s_mm.h, cp_r_mm)
        cp_wo = wload(t["wslot"][:, 0, :, :], _w3(t["wo"]))

        tensor.wait_ge(s_w.h, cp_wkt)
        for dj in range(8):
            g = gp_next()
            cp = gp_mm(g, [(t["wslot"][:, 1, cc, dj * 128:(dj + 1) * 128], t["xk_t"][:, cc, :])
                           for cc in range(8)])
            vector.wait_ge(s_mm.h, cp)
            vector.tensor_scalar_max(t["k1"][:, :], GP[g], 0.0)
            i = vector.tensor_mul(t["kact_t"][:, dj, :], t["k1"][:, :], t["k1"][:, :])
            gp_state["free"][g] = s_ve.inc(i)
        cp_k_mm = s_mm.n
        wslot_cons[1] = cp_k_mm
        mix_gate[0] = cp_k_mm

        tensor.wait_ge(s_w.h, cp_wvt_res)
        for dj in range(8):
            g = gp_next()
            cp = gp_mm(g, [(t["wv_t"][:, cc, dj * 128:(dj + 1) * 128], t["kact_t"][:, cc, :])
                           for cc in range(8)])
            vector.wait_ge(s_mm.h, cp)
            i = vector.tensor_tensor(t["ffncol"][:, dj, :], GP[g], t["r_t"][:, dj, :], ALU.mult)
            gp_state["free"][g] = s_ve.inc(i)

        if os.environ.get("MKDBG") and qt == 0:
            sync.wait_ge(s_ve.h, s_ve.n)
            sync.dma_start(out=t["dbg_ffn"][:, :],
                           in_=t["ffncol"][:, :, :].rearrange("p a b -> p (a b)")).then_inc(s_out.h, 16)
            s_out.n += 16

        sync.wait_ge(s_mm.h, norm_state["bc"])
        for j in range(4):
            sync.wait_ge(s_ve.h, xslot_free[j])
            sync.dma_start(out=t["xtm"][:, j, :],
                           in_=t["xb"][q0 + j * 128: q0 + (j + 1) * 128, :]).then_inc(s_x.h, 16)
            s_x.n += 16
        cp_xrr = s_x.n

        cp_out_prev = s_out.n
        vector.wait_ge(s_out.h, cp_out_prev)
        tensor.wait_ge(s_ve.h, cp_attn)
        tensor.wait_ge(s_kw.h, norm_state["sh"])
        tensor.wait_ge(s_w.h, cp_wo)
        add_cp = {}
        comb_cp = {-1: 0}
        cpT = 0
        for fo in range(9):
            if fo < 8:
                g = 1 + (fo % 6)
                cpm = gp_mm(g, [(t["wslot"][:, 0, cc, fo * 128:(fo + 1) * 128],
                                 t["attncol"][:, cc, :]) for cc in range(8)])
            if fo >= 1:
                p = fo - 1
                tensor.wait_ge(s_ve.h, add_cp[p])
                tensor.wait_ge(s_ve.h, comb_cp[p - 1])
                for j in range(4):
                    i = tensor.transpose(t["G1"][:, j * 128:(j + 1) * 128],
                                         t["k1"][:, j * 128:(j + 1) * 128], t["idf"][:, :])
                cpT = s_mm.inc(i)
            if fo < 8:
                vector.wait_ge(s_mm.h, max(cpm, cpT))
                i = vector.tensor_tensor(t["k1"][:, :], GP[g], t["ffncol"][:, fo, :], ALU.add)
                add_cp[fo] = s_ve.inc(i)
                gp_state["free"][g] = add_cp[fo]
            if fo >= 1:
                p = fo - 1
                vector.wait_ge(s_mm.h, cpT)
                if p == 0:
                    vector.wait_ge(s_x.h, cp_xrr)
                i = vector.tensor_tensor(
                    t["outst"][:, :, p * 128:(p + 1) * 128],
                    t["G1"][:, :].rearrange("p (j f) -> p j f", j=4),
                    t["xtm"][:, :, p * 128:(p + 1) * 128], ALU.add)
                comb_cp[p] = s_ve.inc(i)
        wslot_cons[0] = s_mm.n
        cp_comb = s_ve.n
        norm_state["ve"] = cp_comb
        gp_state["free"][0] = cp_comb
        sync.wait_ge(s_ve.h, cp_comb)
        sync.dma_start(
            out=t["out"][qt * 512:(qt + 1) * 512, :].rearrange("(j p) f -> p j f", p=128),
            in_=t["outst"][:, :, :]).then_inc(s_out.h, 16)
        s_out.n += 16
        for j in range(4):
            xslot_free[j] = cp_comb

    sync.wait_ge(s_out.h, s_out.n)


_NC_CACHE = [None]


def kernel(L0_attn_x, L0_attn_w_qkv, L0_attn_w_o, L0_ffn_W_r, L0_ffn_W_k,
           L0_ffn_W_v, L0_ffn_mu_r, L0_ffn_mu_k):
    x = np.ascontiguousarray(np.asarray(L0_attn_x, dtype=np.float32))
    w_qkv = np.asarray(L0_attn_w_qkv, dtype=np.float32)
    NH, HD = 16, 64
    qkv3 = w_qkv.reshape(D, NH, 3 * HD)
    wqh = np.ascontiguousarray(qkv3[:, :, 0:HD].reshape(D, D))
    wkh = np.ascontiguousarray(qkv3[:, :, HD:2 * HD].reshape(D, D))
    wvh = np.ascontiguousarray(qkv3[:, :, 2 * HD:3 * HD].reshape(D, D))
    wrt = np.ascontiguousarray(np.asarray(L0_ffn_W_r, np.float32).T)
    wkt = np.ascontiguousarray(np.asarray(L0_ffn_W_k, np.float32).T)
    wvt = np.ascontiguousarray(np.asarray(L0_ffn_W_v, np.float32).T)
    wo_ = np.ascontiguousarray(np.asarray(L0_attn_w_o, np.float32))
    mu_ = np.ascontiguousarray(
        np.stack([np.asarray(L0_ffn_mu_r, np.float32), np.asarray(L0_ffn_mu_k, np.float32)]))

    if _NC_CACHE[0] is None:
        _NC_CACHE[0] = build()
    nc = _NC_CACHE[0]

    in_maps = []
    for i in range(8):
        b, p = i // 4, i % 4
        a0, b0 = p * CH, (7 - p) * CH
        xbound = np.zeros((2, D), np.float32)
        if a0 > 0:
            xbound[0] = x[b, a0 - 1]
        xbound[1] = x[b, b0 - 1]
        in_maps.append(dict(
            xb=x[b], xbound=xbound, wqh=wqh, wkh=wkh, wvh=wvh,
            wrt=wrt, wkt=wkt, wvt=wvt, wo=wo_, mu=mu_))
    res = run_bass_kernel_spmd(nc, in_maps, core_ids=list(range(8)))
    outp = np.empty((2, T, D), np.float32)
    for i in range(8):
        b, p = i // 4, i % 4
        o = res.results[i]["out"]
        outp[b, p * CH:(p + 1) * CH] = o[0:CH]
        outp[b, (7 - p) * CH:(8 - p) * CH] = o[CH:2 * CH]
    return outp
